# version 06
# baseline (speedup 1.0000x reference)
import contextlib
import numpy as np

import concourse.bass as bass
import concourse.mybir as mybir
from concourse.bass_utils import run_bass_kernel_spmd
from concourse.aot_env import allow_ici_all_to_all

F32 = mybir.dt.float32
BF16 = mybir.dt.bfloat16
AF = mybir.ActivationFunctionType
ALU = mybir.AluOpType
AX = mybir.AxisListType

NCORES = 8
ENGINES = ("tensor", "vector", "scalar", "gpsimd", "sync")
QR = 12
_QBASE = ("dS", "dA", "dG", "cc")
QUEUES = tuple(f"{q}#{i}" for q in _QBASE for i in range(QR))
_QNEXT = {q: 0 for q in _QBASE}


def _qsub(queue):
    i = _QNEXT[queue]
    _QNEXT[queue] = (i + 1) % QR
    return f"{queue}#{i}"


class Buf:
    __slots__ = ("last_w", "readers")

    def __init__(self):
        self.last_w = None
        self.readers = {}


class Rec:

    def __init__(self, full_edges=False):
        self.full_edges = full_edges
        self.ops = []
        self.seq = {s: 0 for s in ENGINES + QUEUES}
        self.need = {s: set() for s in ENGINES + QUEUES}
        self.bar = {s: 0 for s in ENGINES + QUEUES}

    def barrier(self):
        self.bar = dict(self.seq)

    def op(self, eng, fn, reads=(), writes=(), queue=None):
        stream = _qsub(queue) if queue is not None else eng
        waits = {}
        for s, q in self.bar.items():
            if s != stream and q > 0:
                waits[s] = q
        if queue is not None and self.seq[stream] > 0:
            waits[stream] = self.seq[stream]
        fe = self.full_edges
        for b in reads:
            lw = b.last_w
            if lw is not None and (fe or lw[0] != stream) and waits.get(lw[0], -1) < lw[1]:
                waits[lw[0]] = lw[1]
        for b in writes:
            lw = b.last_w
            if lw is not None and (fe or lw[0] != stream) and waits.get(lw[0], -1) < lw[1]:
                waits[lw[0]] = lw[1]
            for s, q in b.readers.items():
                if (fe or s != stream) and waits.get(s, -1) < q:
                    waits[s] = q
        self.seq[stream] += 1
        myseq = self.seq[stream]
        for b in reads:
            if b.readers.get(stream, -1) < myseq:
                b.readers[stream] = myseq
        for b in writes:
            b.last_w = (stream, myseq)
            b.readers = {}
        for s, q in waits.items():
            self.need[s].add(q)
        self.ops.append((eng, fn, waits, stream, myseq))

    def finalize(self):
        self.cnt = {}
        self.inc = {}
        for s in ENGINES + QUEUES:
            need = self.need[s]
            always = s not in ENGINES
            cnt = [0] * (self.seq[s] + 1)
            inc = [False] * (self.seq[s] + 1)
            c = 0
            for q in range(1, self.seq[s] + 1):
                if always or q in need:
                    c += 1
                    inc[q] = True
                cnt[q] = c
            self.cnt[s] = cnt
            self.inc[s] = inc

    def emit(self, eng_name, proxy, sems):
        wm = {s: 0 for s in ENGINES + QUEUES}
        for eng, fn, waits, stream, myseq in self.ops:
            if eng != eng_name:
                continue
            for s in sorted(waits):
                tgt = self.cnt[s][waits[s]]
                if tgt > wm[s]:
                    step = 16 if s.startswith("d") else 1
                    proxy.wait_ge(sems[s], tgt * step)
                    wm[s] = tgt
            inst = fn(proxy)
            if self.inc[stream][myseq]:
                step = 16 if stream.startswith("d") else 1
                inst.then_inc(sems[stream], step)


def build_nc(B=2, T=8192, debug=False, paranoid=False):
    D = 2048
    INTER = 8192
    L = B * T // NCORES
    SEG = 512
    CH = 128
    NDCH = D // 128
    NICH = INTER // 128
    NT5 = L // 512
    assert T % SEG == 0 and L % 512 == 0

    nc = bass.Bass("TRN2", target_bir_lowering=False,
                   detect_race_conditions=paranoid)

    def din(name, shape, dt=F32):
        return nc.dram_tensor(name, list(shape), dt, kind="ExternalInput").ap()

    x0f = din("x0f", (D, B * T))
    x0loc = din("x0loc", (D, L))
    wq0 = din("wq0", (D, 256)); wk0 = din("wk0", (D, 256))
    wv0 = din("wv0", (D, 256)); wg0 = din("wg0", (D, 256))
    ws1t = din("ws1t", (D, 16)); ws2t = din("ws2t", (16, 128))
    bs2c = din("bs2c", (1, 128)); gnw = din("gnw", (128, 1))
    wo0_tl = din("wo0_tl", (NDCH, 128, D))
    wup_tl = din("wup_tl", (NICH, 128, D))
    wgt_tl = din("wgt_tl", (NICH, 128, D))
    wdn_t = din("wdn_t", (INTER, D))
    su_ug_fm = din("su_ug_fm", (128, 16))
    su_dp_fm = din("su_dp_fm", (128, 64))
    wsc_up_fm = din("wsc_up_fm", (128, 64))
    wsc_gt_fm = din("wsc_gt_fm", (128, 64))
    wsc_dn_fm = din("wsc_dn_fm", (128, 16))
    wq1 = din("wq1", (D, 256)); wk1 = din("wk1", (D, 256)); wv1 = din("wv1", (D, 256))
    wo1_tl = din("wo1_tl", (NDCH, 128, D))
    w1_tl = din("w1_tl", (NICH, 128, D))
    w3_tl = din("w3_tl", (NICH, 128, D))
    w2_t = din("w2_t", (INTER, D))
    cosT = din("cosT", (128, T)); sinT = din("sinT", (128, T))
    cosqT = din("cosqT", (128, T)); sinqT = din("sinqT", (128, T))
    ident_in = din("ident_in", (128, 128))
    hmaskT_in = din("hmaskT_in", (128, 128))
    masks4_in = din("masks4_in", (128, 4 * 512))
    h128_in = din("h128_in", (128, 128))

    y = nc.dram_tensor("y", [D, L], F32, kind="ExternalOutput").ap()

    def dscr(name, shape, dt=F32):
        return nc.dram_tensor(name, list(shape), dt).ap()

    a2a1_in = dscr("a2a1_in", (NCORES * 256, L))
    a2a1_out = dscr("a2a1_out", (NCORES * 256, L))
    ag2_in = dscr("ag2_in", (D, L))
    ag2_out = dscr("ag2_out", (NCORES * D, L))
    a2a3_in = dscr("a2a3_in", (NCORES * 256, L))
    a2a3_out = dscr("a2a3_out", (NCORES * 256, L))
    x1a_s = dscr("x1a_s", (D, L))
    x1loc_s = dscr("x1loc_s", (D, L))
    x2a_s = dscr("x2a_s", (D, L))
    hid_h = dscr("hid_h", (NICH, 128, L), BF16)
    hid_l = dscr("hid_l", (NICH, 128, L), BF16)
    vdn_h = dscr("vdn_h", (16, 128, NICH, 128), BF16)
    vdn_l = dscr("vdn_l", (16, 128, NICH, 128), BF16)
    v2_h = dscr("v2_h", (16, 128, NICH, 128), BF16)
    v2_l = dscr("v2_l", (16, 128, NICH, 128), BF16)

    dbg = {}
    if debug:
        dbg["og"] = nc.dram_tensor("dbg_og", [256, B * T], F32, kind="ExternalOutput").ap()
        dbg["x1a"] = nc.dram_tensor("dbg_x1a", [D, L], F32, kind="ExternalOutput").ap()
        dbg["x1"] = nc.dram_tensor("dbg_x1", [D, L], F32, kind="ExternalOutput").ap()
        dbg["o1"] = nc.dram_tensor("dbg_o1", [256, B * T], F32, kind="ExternalOutput").ap()
        dbg["x2a"] = nc.dram_tensor("dbg_x2a", [D, L], F32, kind="ExternalOutput").ap()
        dbg["vdnh"] = nc.dram_tensor("dbg_vdnh", [16, 128, NICH, 128], BF16, kind="ExternalOutput").ap()
        dbg["vdnl"] = nc.dram_tensor("dbg_vdnl", [16, 128, NICH, 128], BF16, kind="ExternalOutput").ap()
        dbg["hidh"] = nc.dram_tensor("dbg_hidh", [NICH, 128, L], BF16, kind="ExternalOutput").ap()
        dbg["hidl"] = nc.dram_tensor("dbg_hidl", [NICH, 128, L], BF16, kind="ExternalOutput").ap()
        dbg["q0"] = nc.dram_tensor("dbg_q0", [128, 512], F32, kind="ExternalOutput").ap()
        dbg["k0"] = nc.dram_tensor("dbg_k0", [128, 512], F32, kind="ExternalOutput").ap()
        dbg["es0"] = nc.dram_tensor("dbg_es0", [128, 64], F32, kind="ExternalOutput").ap()
        dbg["zz0"] = nc.dram_tensor("dbg_zz0", [128, 64], F32, kind="ExternalOutput").ap()
        dbg["ok0"] = nc.dram_tensor("dbg_ok0", [128, 64], F32, kind="ExternalOutput").ap()
        dbg["qv0"] = nc.dram_tensor("dbg_qv0", [128, 64], F32, kind="ExternalOutput").ap()
        dbg["oo0"] = nc.dram_tensor("dbg_oo0", [128, 512], F32, kind="ExternalOutput").ap()
        dbg["S0"] = nc.dram_tensor("dbg_S0", [128, 64], F32, kind="ExternalOutput").ap()
        dbg["zt0"] = nc.dram_tensor("dbg_zt0", [2048, 512], F32, kind="ExternalOutput").ap()
        dbg["qpre"] = nc.dram_tensor("dbg_qpre", [128, 512], F32, kind="ExternalOutput").ap()
        dbg["tcos"] = nc.dram_tensor("dbg_tcos", [128, 512], F32, kind="ExternalOutput").ap()
        dbg["tsin"] = nc.dram_tensor("dbg_tsin", [128, 512], F32, kind="ExternalOutput").ap()
        dbg["oke"] = nc.dram_tensor("dbg_oke", [128, 64], F32, kind="ExternalOutput").ap()
        dbg["mxc"] = nc.dram_tensor("dbg_mxc", [128, 1], F32, kind="ExternalOutput").ap()
        dbg["smc"] = nc.dram_tensor("dbg_smc", [128, 1], F32, kind="ExternalOutput").ap()
        dbg["enz0"] = nc.dram_tensor("dbg_enz0", [128, 64], F32, kind="ExternalOutput").ap()

    rec = Rec(full_edges=paranoid)
    stk = contextlib.ExitStack()
    uid = [0]

    def nm(p):
        uid[0] += 1
        return f"{p}{uid[0]}"

    def sb(name, shape, dt=F32):
        return stk.enter_context(nc.sbuf_tensor(name, list(shape), dt))

    def load(dst_ap, src_ap, wbuf, rbufs=()):
        rec.op("sync", lambda e, d=dst_ap, s=src_ap: e.dma_start(out=d, in_=s),
               reads=rbufs, writes=(wbuf,), queue="dS")

    def store(dst_ap, src_ap, rbuf, wbuf=None):
        rec.op("scalar", lambda e, d=dst_ap, s=src_ap: e.dma_start(out=d, in_=s),
               reads=(rbuf,), writes=(wbuf,) if wbuf is not None else (),
               queue="dA")

    def vop(fn, reads=(), writes=()):
        rec.op("vector", fn, reads=reads, writes=writes)

    def sop(fn, reads=(), writes=()):
        rec.op("scalar", fn, reads=reads, writes=writes)

    def top(fn, reads=(), writes=()):
        rec.op("tensor", fn, reads=reads, writes=writes)

    def split(hi_ap, lo_ap, src_ap, src_bufs, pbuf):
        sop(lambda e, h=hi_ap, s=src_ap: e.activation(h, s, AF.Copy),
            reads=tuple(src_bufs), writes=(pbuf,))
        vop(lambda e, l=lo_ap, s=src_ap, h=hi_ap: e.tensor_sub(l, s, h),
            reads=tuple(src_bufs) + (pbuf,), writes=(pbuf,))

    def mm3(ps_ap, lh, ll, rh, rl, ps_buf, lbuf, rbuf, start, stop):
        top(lambda e: e.matmul(ps_ap, lh, rh, start=start, stop=False),
            reads=(lbuf, rbuf), writes=(ps_buf,))
        top(lambda e: e.matmul(ps_ap, lh, rl, start=False, stop=False),
            reads=(lbuf, rbuf), writes=(ps_buf,))
        top(lambda e: e.matmul(ps_ap, ll, rh, start=False, stop=stop),
            reads=(lbuf, rbuf), writes=(ps_buf,))

    def mm2s(ps_ap, lx, rh, rl, ps_buf, lbuf, rbuf, start, stop):
        top(lambda e: e.matmul(ps_ap, lx, rh, start=start, stop=False),
            reads=(lbuf, rbuf), writes=(ps_buf,))
        top(lambda e: e.matmul(ps_ap, lx, rl, start=False, stop=stop),
            reads=(lbuf, rbuf), writes=(ps_buf,))

    pbk = [stk.enter_context(nc.psum_tensor(f"pb{i}", [128, 512], F32))
           for i in range(8)]
    bp = [Buf() for _ in range(8)]

    ones_row = sb("ones_row", (1, 128), BF16)
    ones_col = sb("ones_col", (128, 1), BF16)
    ident_b = sb("ident_b", (128, 128), BF16)
    hmaskT = sb("hmaskT", (128, 128), BF16)
    masks4 = sb("masks4", (128, 4 * 512), BF16)
    h128su = sb("h128su", (128, 16 * 128), BF16)
    h128b = sb("h128b", (128, 128), BF16)
    gnw_c = sb("gnw_c", (128, 1), F32)
    cs_up = sb("cs_up", (128, 64), F32)
    cs_gt = sb("cs_gt", (128, 64), F32)
    cs_dn = sb("cs_dn", (128, 16), F32)
    su_dp = sb("su_dp", (128, 64), F32)
    c_eps5 = sb("c_eps5", (128, 1), F32)
    c_rms = sb("c_rms", (128, 1), F32)
    tmpg = sb("tmpg", (128, 512), F32)
    bO = Buf()
    bT_ = Buf()
    bA2A1 = Buf(); bAG2 = Buf(); bA2A3 = Buf()
    bIN1 = Buf(); bIN2 = Buf(); bIN3 = Buf()
    bX1A = Buf(); bX1L = Buf(); bX2A = Buf()
    bHID = Buf(); bVDN = Buf(); bV2 = Buf(); bY = Buf()

    def emit_consts():
        vop(lambda e: e.memset(ones_row[:], 1.0), writes=(bO,))
        vop(lambda e: e.memset(ones_col[:], 1.0), writes=(bO,))
        vop(lambda e: e.memset(c_eps5[:], 1e-5), writes=(bO,))
        vop(lambda e: e.memset(c_rms[:], 1.0 / 128.0), writes=(bO,))
        load(tmpg[:, 0:128], ident_in, bT_, rbufs=(bT_,))
        vop(lambda e: e.tensor_copy(ident_b[:], tmpg[:, 0:128]), reads=(bT_,), writes=(bO,))
        load(tmpg[:, 0:128], hmaskT_in, bT_, rbufs=(bT_,))
        vop(lambda e: e.tensor_copy(hmaskT[:], tmpg[:, 0:128]), reads=(bT_,), writes=(bO,))
        load(gnw_c[:], gnw, bO)
        for mi in range(4):
            load(tmpg[:], masks4_in[:, mi * 512:(mi + 1) * 512], bT_, rbufs=(bT_,))
            vop(lambda e, mi=mi: e.tensor_copy(masks4[:, mi * 512:(mi + 1) * 512], tmpg[:]),
                reads=(bT_,), writes=(bO,))
        load(tmpg[:, 0:128], h128_in, bT_, rbufs=(bT_,))
        load(tmpg[:, 128:144], su_ug_fm, bT_, rbufs=(bT_,))
        vop(lambda e: e.tensor_copy(h128b[:], tmpg[:, 0:128]), reads=(bT_,), writes=(bO,))
        for c in range(16):
            vop(lambda e, c=c: e.tensor_scalar(
                h128su[:, c * 128:(c + 1) * 128], tmpg[:, 0:128],
                tmpg[:, 128 + c:129 + c], None, ALU.mult),
                reads=(bT_,), writes=(bO,))
        load(su_dp[:], su_dp_fm, bO)
        load(cs_up[:], wsc_up_fm, bO)
        load(cs_gt[:], wsc_gt_fm, bO)
        load(cs_dn[:], wsc_dn_fm, bO)
        s2048 = 1.0 / float(np.sqrt(2048.0))
        s8192 = 1.0 / float(np.sqrt(8192.0))
        vop(lambda e: e.tensor_scalar_mul(cs_up[:], cs_up[:], s2048), reads=(bO,), writes=(bO,))
        vop(lambda e: e.tensor_scalar_mul(cs_gt[:], cs_gt[:], s2048), reads=(bO,), writes=(bO,))
        vop(lambda e: e.tensor_scalar_mul(cs_dn[:], cs_dn[:], s8192), reads=(bO,), writes=(bO,))

    def emit_vprep(wsrc, vh, vl, signed, vbuf):
        CB = 512
        with contextlib.ExitStack() as ps:
            sbl = lambda n, sh, dt=F32: ps.enter_context(nc.sbuf_tensor(nm(n), list(sh), dt))
            a_t = [sbl("vpa", (128, CB)) for _ in range(NICH)]
            ba = [Buf() for _ in range(NICH)]
            tmp = sbl("vptmp", (128, CB))
            btmp = Buf()
            wld = [sbl("vpw", (128, CB)) for _ in range(3)]
            wph = [sbl("vph", (128, CB), BF16) for _ in range(3)]
            wpl = [sbl("vpl", (128, CB), BF16) for _ in range(3)]
            bw = [Buf() for _ in range(3)]
            bwp = [Buf() for _ in range(3)]
            oh = [sbl("voh", (128, CB), BF16) for _ in range(2)]
            ol = [sbl("vol", (128, CB), BF16) for _ in range(2)]
            boh = [Buf() for _ in range(2)]
            for cb in range(D // CB):
                for ic in range(NICH):
                    k = ic % 3
                    load(wld[k][:], wsrc[ic * 128:(ic + 1) * 128, cb * CB:(cb + 1) * CB],
                         bw[k], rbufs=(bw[k], bwp[k]))
                    split(wph[k][:], wpl[k][:], wld[k][:], (bw[k],), bwp[k])
                    mm2s(pbk[0][:, 0:CB], h128b[:], wph[k][:], wpl[k][:],
                         bp[0], bO, bwp[k], True, True)
                    sop(lambda e, d=a_t[ic]: e.copy(d[:], pbk[0][:, 0:CB]),
                        reads=(bp[0],), writes=(ba[ic],))
                h = 1
                while h < NICH:
                    for blk in range(0, NICH, 2 * h):
                        for li in range(h):
                            i0, i1 = blk + li, blk + li + h
                            vop(lambda e, a=a_t[i0], b=a_t[i1]: e.tensor_add(tmp[:], a[:], b[:]),
                                reads=(ba[i0], ba[i1]), writes=(btmp,))
                            vop(lambda e, a=a_t[i0], b=a_t[i1]: e.tensor_sub(b[:], a[:], b[:]),
                                reads=(ba[i0], ba[i1]), writes=(ba[i1],))
                            vop(lambda e, a=a_t[i0]: e.tensor_copy(a[:], tmp[:]),
                                reads=(btmp,), writes=(ba[i0],))
                    h *= 2
                for ic in range(NICH):
                    k = ic % 2
                    if signed:
                        vop(lambda e, s=a_t[ic], ic=ic: e.tensor_scalar(
                            s[:], s[:], su_dp[:, ic:ic + 1], None, ALU.mult),
                            reads=(ba[ic], bO), writes=(ba[ic],))
                    split(oh[k][:], ol[k][:], a_t[ic][:], (ba[ic],), boh[k])
                    for sl in range(CB // 128):
                        oc = cb * (CB // 128) + sl
                        store(vh[oc, :, ic, :], oh[k][:, sl * 128:(sl + 1) * 128],
                              boh[k], wbuf=vbuf)
                        store(vl[oc, :, ic, :], ol[k][:, sl * 128:(sl + 1) * 128],
                              boh[k], wbuf=vbuf)

    def emit_p0():
        NSEG = T // SEG
        NCHS = SEG // CH
        with contextlib.ExitStack() as ps:
            sbl = lambda n, sh, dt=F32: ps.enter_context(nc.sbuf_tensor(nm(n), list(sh), dt))
            wsb = {}
            bW = Buf()
            ldw = sbl("p0wld", (128, 256))
            bldw = Buf()
            for pname in ("q", "k", "v", "g"):
                for hl in range(2):
                    wsb[(pname, hl)] = (sbl("wsh", (128, NDCH * 128), BF16),
                                        sbl("wsl", (128, NDCH * 128), BF16))
            ws1p = (sbl("ws1h", (128, NDCH * 16), BF16), sbl("ws1l", (128, NDCH * 16), BF16))
            ws2p = (sbl("ws2h", (16, 128), BF16), sbl("ws2l", (16, 128), BF16))
            bs2p = (sbl("bs2h", (1, 128), BF16), sbl("bs2l", (1, 128), BF16))
            for pname, src in (("q", wq0), ("k", wk0), ("v", wv0), ("g", wg0)):
                for ic in range(NDCH):
                    load(ldw[:], src[ic * 128:(ic + 1) * 128, :], bldw, rbufs=(bldw, bW))
                    for hl in range(2):
                        hi_t, lo_t = wsb[(pname, hl)]
                        split(hi_t[:, ic * 128:(ic + 1) * 128],
                              lo_t[:, ic * 128:(ic + 1) * 128],
                              ldw[:, hl * 128:(hl + 1) * 128], (bldw,), bW)
            for ic in range(NDCH):
                load(ldw[:, 0:16], ws1t[ic * 128:(ic + 1) * 128, :], bldw, rbufs=(bldw, bW))
                split(ws1p[0][:, ic * 16:(ic + 1) * 16], ws1p[1][:, ic * 16:(ic + 1) * 16],
                      ldw[:, 0:16], (bldw,), bW)
            load(ldw[0:16, 0:128], ws2t, bldw, rbufs=(bldw, bW))
            split(ws2p[0][:], ws2p[1][:], ldw[0:16, 0:128], (bldw,), bW)
            load(ldw[0:1, 0:128], bs2c, bldw, rbufs=(bldw, bW))
            split(bs2p[0][:], bs2p[1][:], ldw[0:1, 0:128], (bldw,), bW)

            xx = sbl("p0xld", (128, SEG))
            bxl = Buf()
            xp = [(sbl("p0xh", (128, SEG), BF16), sbl("p0xl", (128, SEG), BF16))
                  for _ in range(NDCH)]
            bx = [Buf() for _ in range(NDCH)]
            rope = sbl("p0rope", (128, 4 * SEG))
            brope4 = [Buf() for _ in range(4)]
            qp = [(sbl("qh", (128, SEG), BF16), sbl("ql", (128, SEG), BF16)) for _ in range(2)]
            kp = [(sbl("kh", (128, SEG), BF16), sbl("kl", (128, SEG), BF16)) for _ in range(2)]
            vtm = [(sbl("vth", (128, SEG), BF16), sbl("vtl", (128, SEG), BF16)) for _ in range(2)]
            gg = [sbl("gg", (128, SEG)) for _ in range(2)]
            oo = [sbl("oo", (128, SEG)) for _ in range(2)]
            bq = [Buf(), Buf()]; bk = [Buf(), Buf()]; bv = [Buf(), Buf()]
            bg = [Buf(), Buf()]; bo = [Buf(), Buf()]
            up = (sbl("p0uh", (16, T), BF16), sbl("p0ul", (16, T), BF16))
            bu = Buf()
            vfp = (sbl("p0vfh", (128, SEG), BF16), sbl("p0vfl", (128, SEG), BF16))
            bvf = Buf()
            rop_t = sbl("p0ropt", (128, SEG))
            rot2 = sbl("p0rot2", (128, SEG))
            brt = Buf()
            brot = Buf()
            Sst = [[sbl("Ss", (128, 64)) for _ in range(2)] for _ in range(2)]
            Ust = [[sbl("Us", (64, 128)) for _ in range(2)] for _ in range(2)]
            bSst = [[Buf(), Buf()], [Buf(), Buf()]]
            bUst = [[Buf(), Buf()], [Buf(), Buf()]]
            Sp = [(sbl("Sph", (128, 64), BF16), sbl("Spl", (128, 64), BF16)) for _ in range(2)]
            Up = [(sbl("Uph", (64, 128), BF16), sbl("Upl", (64, 128), BF16)) for _ in range(2)]
            bSp = [Buf(), Buf()]; bUp = [Buf(), Buf()]
            zcar = [(sbl("zch", (1, 64), BF16), sbl("zcl", (1, 64), BF16)) for _ in range(2)]
            bzc = [Buf(), Buf()]
            es_tm = [sbl("es_tm", (128, 64)) for _ in range(2)]
            es_tp = [(sbl("es_tph", (128, 64), BF16), sbl("es_tpl", (128, 64), BF16))
                     for _ in range(2)]
            es_fp = [(sbl("es_fph", (64, 128), BF16), sbl("es_fpl", (64, 128), BF16))
                     for _ in range(2)]
            bes = [Buf(), Buf()]; besf = [Buf(), Buf()]
            at_m = [sbl("at_m", (128, 128)) for _ in range(2)]
            at_p = [(sbl("at_ph", (128, 128), BF16), sbl("at_pl", (128, 128), BF16))
                    for _ in range(2)]
            bat = [Buf(), Buf()]
            w2_p = [(sbl("w2_ph", (128, 128), BF16), sbl("w2_pl", (128, 128), BF16))
                    for _ in range(2)]
            bw2 = [Buf(), Buf()]
            ktm = [(sbl("ktm_h", (128, 128), BF16), sbl("ktm_l", (128, 128), BF16))
                   for _ in range(2)]
            bktm = [Buf(), Buf()]
            zz = [sbl("zz", (128, 64)) for _ in range(2)]
            enz = [sbl("enz", (128, 64)) for _ in range(2)]
            bz = [Buf(), Buf()]; benz = [Buf(), Buf()]
            okv = [sbl("okv", (128, 64)) for _ in range(2)]
            bok = [Buf(), Buf()]
            mxc = [sbl("mxc", (128, 1)) for _ in range(2)]
            smc = [sbl("smc", (128, 1)) for _ in range(2)]
            qden = [sbl("qden", (128, 64)) for _ in range(2)]
            qden2 = [sbl("qden2", (128, 64)) for _ in range(2)]
            bmx = [Buf(), Buf()]
            bqd = [Buf(), Buf()]
            qvp = [sbl("qvp", (128, 64)) for _ in range(2)]
            qv_tp = [(sbl("qv_th", (128, 64), BF16), sbl("qv_tl", (128, 64), BF16))
                     for _ in range(2)]
            qv_fp = [(sbl("qv_fh", (64, 128), BF16), sbl("qv_fl", (64, 128), BF16))
                     for _ in range(2)]
            bqv = [Buf(), Buf()]; bqvf = [Buf(), Buf()]
            o2t = sbl("o2t", (128, SEG))
            bo2 = Buf()
            rrow = sbl("p0rrow", (1, SEG))
            rrow2 = sbl("p0rrow2", (1, SEG))
            brr = Buf()
            sgt = sbl("p0sg", (128, SEG))
            bsg = Buf()
            sp2 = (sbl("sp2h", (128, SEG), BF16), sbl("sp2l", (128, SEG), BF16))
            bsp2 = Buf()
            rrp = (sbl("rrph", (1, SEG), BF16), sbl("rrpl", (1, SEG), BF16))
            brrp = Buf()

            for b in range(B):
                for seg in range(NSEG):
                    t0 = seg * SEG
                    g0 = b * T + t0
                    ib = seg % 2
                    ir = (seg + 1) % 2
                    for ic in range(NDCH):
                        load(xx[:], x0f[ic * 128:(ic + 1) * 128, g0:g0 + SEG], bxl,
                             rbufs=(bxl, bx[ic]))
                        split(xp[ic][0][:], xp[ic][1][:], xx[:], (bxl,), bx[ic])
                    load(rope[:, 0:SEG], cosT[:, t0:t0 + SEG], brope4[0], rbufs=(brope4[0],))
                    load(rope[:, SEG:2 * SEG], sinT[:, t0:t0 + SEG], brope4[1], rbufs=(brope4[1],))
                    load(rope[:, 2 * SEG:3 * SEG], cosqT[:, t0:t0 + SEG], brope4[2], rbufs=(brope4[2],))
                    load(rope[:, 3 * SEG:4 * SEG], sinqT[:, t0:t0 + SEG], brope4[3], rbufs=(brope4[3],))
                    for hl in range(2):
                        for ic in range(NDCH):
                            st, sp_ = ic == 0, ic == NDCH - 1
                            for pj, pbi in (("q", 0), ("k", 1), ("v", 2), ("g", 3)):
                                w = wsb[(pj, hl)]
                                mm3(pbk[pbi][:, 0:SEG],
                                    w[0][:, ic * 128:(ic + 1) * 128],
                                    w[1][:, ic * 128:(ic + 1) * 128],
                                    xp[ic][0][:], xp[ic][1][:],
                                    bp[pbi], bW, bx[ic], st, sp_)
                            if hl == 0:
                                mm3(pbk[4][0:16, 0:SEG],
                                    ws1p[0][:, ic * 16:(ic + 1) * 16],
                                    ws1p[1][:, ic * 16:(ic + 1) * 16],
                                    xp[ic][0][:], xp[ic][1][:],
                                    bp[4], bW, bx[ic], st, sp_)
                        if hl == 0:
                            split(up[0][0:16, t0:t0 + SEG], up[1][0:16, t0:t0 + SEG],
                                  pbk[4][0:16, 0:SEG], (bp[4],), bu)
                        for pbi, dstp, bdst, ci, si in (
                                (0, qp[hl], bq[hl], 2 * SEG, 3 * SEG),
                                (1, kp[hl], bk[hl], 0, SEG)):
                            cosap = rope[:, ci:ci + SEG]
                            sinap = rope[:, si:si + SEG]
                            bc = brope4[2] if pbi == 0 else brope4[0]
                            bs_ = brope4[3] if pbi == 0 else brope4[1]
                            vop(lambda e, i=pbi, c=cosap: e.tensor_tensor(
                                rop_t[:], pbk[i][:, 0:SEG], c, ALU.mult),
                                reads=(bp[pbi], bc), writes=(brt,))
                            vop(lambda e, i=pbi, s=sinap: e.tensor_tensor(
                                pbk[6][:, 0:SEG], pbk[i][:, 0:SEG], s, ALU.mult),
                                reads=(bp[pbi], bs_), writes=(bp[6],))
                            if debug == "full" and b == 0 and seg == 0 and hl == 0 and pbi == 0:
                                sop(lambda e, i=pbi: e.copy(rot2[:], pbk[i][:, 0:SEG]),
                                    reads=(bp[pbi],), writes=(brot,))
                                store(dbg["qpre"][:, 0:SEG], rot2[:], brot)
                                store(dbg["tcos"][:, 0:SEG], rop_t[:], brt)
                                sop(lambda e: e.copy(rot2[:], pbk[6][:, 0:SEG]),
                                    reads=(bp[6],), writes=(brot,))
                                store(dbg["tsin"][:, 0:SEG], rot2[:], brot)
                            vop(lambda e: e.tensor_sub(
                                rot2[0:64, :], rop_t[0:64, :], pbk[6][64:128, 0:SEG]),
                                reads=(brt, bp[6]), writes=(brot,))
                            vop(lambda e: e.tensor_add(
                                rot2[64:128, :], rop_t[64:128, :], pbk[6][0:64, 0:SEG]),
                                reads=(brt, bp[6]), writes=(brot,))
                            split(dstp[0][:], dstp[1][:], rot2[:], (brot,), bdst)
                            if debug == "full" and b == 0 and seg == 0 and hl == 0:
                                key = "q0" if pbi == 0 else "k0"
                                store(dbg[key][:, 0:SEG], rot2[:], brot)
                        split(vfp[0][:], vfp[1][:], pbk[2][:, 0:SEG], (bp[2],), bvf)
                        for q4 in range(SEG // 128):
                            for ip in range(2):
                                top(lambda e, ip=ip, q4=q4: e.transpose(
                                    pbk[5][:, 0:64].bitcast(BF16),
                                    vfp[ip][:, q4 * 128:(q4 + 1) * 128], ident_b[:]),
                                    reads=(bvf, bO), writes=(bp[5],))
                                vop(lambda e, ip=ip, q4=q4, hl=hl: e.tensor_copy(
                                    vtm[hl][ip][:, q4 * 128:(q4 + 1) * 128],
                                    pbk[5][:, 0:64].bitcast(BF16)),
                                    reads=(bp[5],), writes=(bv[hl],))
                        sop(lambda e, hl=hl: e.copy(gg[hl][:], pbk[3][:, 0:SEG]),
                            reads=(bp[3],), writes=(bg[hl],))
                    for hl in range(2):
                        for cc in range(NCHS):
                            tl = cc * CH
                            p = cc % 2
                            pS = 4 if p == 0 else 6
                            pW = 2 if p == 0 else 7
                            first = (seg == 0 and cc == 0)
                            qc = (qp[hl][0][:, tl:tl + CH], qp[hl][1][:, tl:tl + CH])
                            kc = (kp[hl][0][:, tl:tl + CH], kp[hl][1][:, tl:tl + CH])
                            vc = (vtm[hl][0][:, tl:tl + CH], vtm[hl][1][:, tl:tl + CH])
                            uc = (up[0][0:16, t0 + tl:t0 + tl + CH],
                                  up[1][0:16, t0 + tl:t0 + tl + CH])
                            mm3(pbk[pS][:, 0:64], uc[0], uc[1],
                                ws2p[0][:, hl * 64:hl * 64 + 64],
                                ws2p[1][:, hl * 64:hl * 64 + 64],
                                bp[pS], bu, bW, True, False)
                            mm2s(pbk[pS][:, 0:64], ones_row[0:1, 0:CH],
                                 bs2p[0][0:1, hl * 64:hl * 64 + 64],
                                 bs2p[1][0:1, hl * 64:hl * 64 + 64],
                                 bp[pS], bO, bW, False, True)
                            vop(lambda e, p=p, pS=pS: e.tensor_scalar(
                                es_tm[p][:], pbk[pS][:, 0:64], 32.0, -32.0,
                                ALU.min, ALU.max),
                                reads=(bp[pS],), writes=(bes[p],))
                            sop(lambda e, p=p: e.activation(es_tm[p][:], es_tm[p][:], AF.Exp),
                                reads=(bes[p],), writes=(bes[p],))
                            split(es_tp[p][0][:], es_tp[p][1][:], es_tm[p][:],
                                  (bes[p],), bes[p])
                            for ip in range(2):
                                top(lambda e, ip=ip, p=p: e.transpose(
                                    pbk[5][0:64, 0:64].bitcast(BF16), es_tp[p][ip][:],
                                    ident_b[:]),
                                    reads=(bes[p], bO), writes=(bp[5],))
                                vop(lambda e, ip=ip, p=p: e.tensor_copy(
                                    es_fp[p][ip][:], pbk[5][0:64, 0:64].bitcast(BF16)),
                                    reads=(bp[5],), writes=(besf[p],))
                            mm2s(pbk[pS][:, 0:64], hmaskT[:], es_tp[p][0][:], es_tp[p][1][:],
                                 bp[pS], bO, bes[p], True, first)
                            if not first:
                                mm2s(pbk[pS][:, 0:64], ones_row[0:1, 0:CH],
                                     zcar[hl][0][:], zcar[hl][1][:], bp[pS], bO, bzc[hl],
                                     False, True)
                            sop(lambda e, p=p, pS=pS: e.copy(zz[p][:], pbk[pS][:, 0:64]),
                                reads=(bp[pS],), writes=(bz[p],))
                            mm2s(pbk[pS][0:1, 0:64], ones_col[:, 0:1],
                                 es_tp[p][0][:], es_tp[p][1][:], bp[pS], bO, bes[p],
                                 True, first)
                            if not first:
                                mm2s(pbk[pS][0:1, 0:64], ones_row[0:1, 0:1],
                                     zcar[hl][0][:], zcar[hl][1][:], bp[pS], bO,
                                     bzc[hl], False, True)
                            split(zcar[hl][0][:], zcar[hl][1][:], pbk[pS][0:1, 0:64],
                                  (bp[pS],), bzc[hl])
                            vop(lambda e, p=p: e.reciprocal(enz[p][:], zz[p][:]),
                                reads=(bz[p],), writes=(benz[p],))
                            mm3(pbk[pW][:, 0:128], kc[0], kc[1], qc[0], qc[1],
                                bp[pW], bk[hl], bq[hl], True, True)
                            vop(lambda e, p=p, pW=pW: e.tensor_tensor(
                                at_m[p][:], pbk[pW][:, 0:128], hmaskT[:], ALU.mult),
                                reads=(bp[pW], bO), writes=(bat[p],))
                            split(at_p[p][0][:], at_p[p][1][:], at_m[p][:],
                                  (bat[p],), bat[p])
                            mm3(pbk[pS][:, 0:64], at_p[p][0][:], at_p[p][1][:],
                                es_tp[p][0][:], es_tp[p][1][:], bp[pS], bat[p], bes[p],
                                True, first)
                            if not first:
                                mm3(pbk[pS][:, 0:64], qc[0], qc[1],
                                    Sp[hl][0][:], Sp[hl][1][:], bp[pS], bq[hl], bSp[hl],
                                    False, True)
                            vop(lambda e, p=p, pS=pS: e.tensor_tensor(
                                okv[p][:], pbk[pS][:, 0:64], enz[p][:], ALU.mult),
                                reads=(bp[pS], benz[p]), writes=(bok[p],))
                            vop(lambda e, p=p: e.reduce_max(mxc[p][:], okv[p][:],
                                                            axis=AX.X, negate=True),
                                reads=(bok[p],), writes=(bmx[p],))
                            sop(lambda e, p=p: e.activation(okv[p][:], okv[p][:], AF.Exp,
                                                            bias=mxc[p][:]),
                                reads=(bok[p], bmx[p]), writes=(bok[p],))
                            vop(lambda e, p=p: e.reduce_sum(smc[p][:], okv[p][:], axis=AX.X),
                                reads=(bok[p],), writes=(bmx[p],))
                            vop(lambda e, p=p: e.tensor_scalar(
                                qden[p][:], zz[p][:], smc[p][:], None, ALU.mult),
                                reads=(bz[p], bmx[p]), writes=(bqd[p],))
                            vop(lambda e, p=p: e.reciprocal(qden2[p][:], qden[p][:]),
                                reads=(bqd[p],), writes=(bqd[p],))
                            vop(lambda e, p=p: e.tensor_mul(qvp[p][:], okv[p][:],
                                                            qden2[p][:]),
                                reads=(bok[p], bqd[p]), writes=(bqv[p],))
                            split(qv_tp[p][0][:], qv_tp[p][1][:], qvp[p][:],
                                  (bqv[p],), bqv[p])
                            for ip in range(2):
                                top(lambda e, ip=ip, p=p: e.transpose(
                                    pbk[5][0:64, 0:64].bitcast(BF16), qv_tp[p][ip][:],
                                    ident_b[:]),
                                    reads=(bqv[p], bO), writes=(bp[5],))
                                vop(lambda e, ip=ip, p=p: e.tensor_copy(
                                    qv_fp[p][ip][:], pbk[5][0:64, 0:64].bitcast(BF16)),
                                    reads=(bp[5],), writes=(bqvf[p],))
                            mm3(pbk[pW][:, 0:128], es_fp[p][0][:], es_fp[p][1][:],
                                qv_fp[p][0][:], qv_fp[p][1][:], bp[pW], besf[p], bqvf[p],
                                True, True)
                            vop(lambda e, p=p, pW=pW: e.tensor_tensor(
                                at_m[p][:], pbk[pW][:, 0:128], hmaskT[:], ALU.mult),
                                reads=(bp[pW], bO), writes=(bw2[p],))
                            split(w2_p[p][0][:], w2_p[p][1][:], at_m[p][:],
                                  (bw2[p],), bw2[p])
                            mm3(pbk[3][:, 0:128], vc[0], vc[1], w2_p[p][0][:], w2_p[p][1][:],
                                bp[3], bv[hl], bw2[p], True, first)
                            if not first:
                                mm3(pbk[3][:, 0:128], Up[hl][0][:], Up[hl][1][:],
                                    qv_fp[p][0][:], qv_fp[p][1][:], bp[3], bUp[hl],
                                    bqvf[p], False, True)
                            sop(lambda e, tl=tl, hl=hl: e.copy(oo[hl][:, tl:tl + CH],
                                                               pbk[3][:, 0:128]),
                                reads=(bp[3],), writes=(bo[hl],))
                            for ip in range(2):
                                top(lambda e, ip=ip, k_=kc: e.transpose(
                                    pbk[5][:, 0:64].bitcast(BF16), k_[ip], ident_b[:]),
                                    reads=(bk[hl], bO), writes=(bp[5],))
                                vop(lambda e, ip=ip, p=p: e.tensor_copy(
                                    ktm[p][ip][:], pbk[5][:, 0:64].bitcast(BF16)),
                                    reads=(bp[5],), writes=(bktm[p],))
                            mm3(pbk[0][:, 0:64], ktm[p][0][:], ktm[p][1][:],
                                es_tp[p][0][:], es_tp[p][1][:], bp[0], bktm[p], bes[p],
                                True, True)
                            mm3(pbk[1][0:64, 0:128], es_tp[p][0][:], es_tp[p][1][:],
                                vc[0], vc[1], bp[1], bes[p], bv[hl], True, True)
                            cidx = seg * NCHS + cc
                            i_b, i_r = cidx % 2, (cidx + 1) % 2
                            if first:
                                sop(lambda e, d=Sst[hl][i_r]: e.copy(d[:], pbk[0][:, 0:64]),
                                    reads=(bp[0],), writes=(bSst[hl][i_r],))
                                sop(lambda e, d=Ust[hl][i_r]: e.copy(d[:], pbk[1][0:64, 0:128]),
                                    reads=(bp[1],), writes=(bUst[hl][i_r],))
                            else:
                                vop(lambda e, d=Sst[hl][i_r], s=Sst[hl][i_b]: e.tensor_tensor(
                                    d[:], s[:], pbk[0][:, 0:64], ALU.add),
                                    reads=(bp[0], bSst[hl][i_b]), writes=(bSst[hl][i_r],))
                                vop(lambda e, d=Ust[hl][i_r], s=Ust[hl][i_b]: e.tensor_tensor(
                                    d[:], s[:], pbk[1][0:64, 0:128], ALU.add),
                                    reads=(bp[1], bUst[hl][i_b]), writes=(bUst[hl][i_r],))
                            split(Sp[hl][0][:], Sp[hl][1][:], Sst[hl][i_r][:],
                                  (bSst[hl][i_r],), bSp[hl])
                            split(Up[hl][0][:], Up[hl][1][:], Ust[hl][i_r][:],
                                  (bUst[hl][i_r],), bUp[hl])
                        sop(lambda e, hl=hl: e.activation(o2t[:], oo[hl][:], AF.Square),
                            reads=(bo[hl],), writes=(bo2,))
                        split(sp2[0][:], sp2[1][:], o2t[:], (bo2,), bsp2)
                        mm2s(pbk[4][0:1, 0:SEG], ones_col[:, 0:1], sp2[0][:], sp2[1][:],
                             bp[4], bO, bsp2, True, True)
                        sop(lambda e: e.activation(rrow[:], pbk[4][0:1, 0:SEG], AF.Sqrt,
                                                   bias=c_eps5[0:1, :], scale=c_rms[0:1, :]),
                            reads=(bp[4], bO), writes=(brr,))
                        vop(lambda e: e.reciprocal(rrow2[:], rrow[:]),
                            reads=(brr,), writes=(brr,))
                        split(rrp[0][:], rrp[1][:], rrow2[:], (brr,), brrp)
                        mm2s(pbk[4][:, 0:SEG], ones_row[0:1, :], rrp[0][:], rrp[1][:],
                             bp[4], bO, brrp, True, True)
                        vop(lambda e, hl=hl: e.scalar_tensor_tensor(
                            o2t[:], oo[hl][:], gnw_c[:, 0:1], pbk[4][:, 0:SEG],
                            ALU.mult, ALU.mult),
                            reads=(bo[hl], bp[4], bO), writes=(bo2,))
                        sop(lambda e, hl=hl: e.activation(sgt[:], gg[hl][:], AF.Silu),
                            reads=(bg[hl],), writes=(bsg,))
                        vop(lambda e: e.tensor_mul(o2t[:], o2t[:], sgt[:]),
                            reads=(bo2, bsg), writes=(bo2,))
                        if debug == "full" and b == 0 and seg == 0 and hl == 0:
                            store(dbg["oo0"][:, 0:SEG], oo[hl][:], bo[hl])
                        blk = g0 // L
                        coff = g0 % L
                        store(a2a1_in[blk * 256 + hl * 128:blk * 256 + (hl + 1) * 128,
                                      coff:coff + SEG], o2t[:], bo2, wbuf=bIN1)
                        if debug:
                            store(dbg["og"][hl * 128:(hl + 1) * 128, g0:g0 + SEG],
                                  o2t[:], bo2)

    def emit_collective(kind, in_ap, out_ap, rbuf, wbuf):
        def fn(e):
            with allow_ici_all_to_all():
                return e.collective_compute(
                    kind, ALU.bypass, replica_groups=[list(range(NCORES))],
                    ins=[in_ap.opt()], outs=[out_ap.opt()])
        rec.op("gpsimd", fn, reads=(rbuf,), writes=(wbuf,), queue="cc")

    def emit_proj_ln(og_src, bog, wo_tl_ap, xres_ap, bxres, xa_dst, bxa, dbg_key):
        with contextlib.ExitStack() as ps:
            sbl = lambda n, sh, dt=F32: ps.enter_context(nc.sbuf_tensor(nm(n), list(sh), dt))
            og_p = [(sbl("ogh", (128, 512), BF16), sbl("ogl", (128, 512), BF16))
                    for _ in range(NDCH)]
            bogp = [Buf() for _ in range(NDCH)]
            ogld = sbl("ogld", (128, 512))
            bold = Buf()
            wsl = [sbl("wosl", (128, D)) for _ in range(2)]
            wsp = [(sbl("woh", (128, D), BF16), sbl("wol", (128, D), BF16))
                   for _ in range(2)]
            bws = [Buf(), Buf()]
            bwsp = [Buf(), Buf()]
            xrt = [sbl("xrt", (128, 512)) for _ in range(2)]
            bxrt = [Buf(), Buf()]
            r_t = [sbl("rt_", (128, 512)) for _ in range(NDCH)]
            br = [Buf() for _ in range(NDCH)]
            rp = (sbl("rph", (128, 512), BF16), sbl("rpl", (128, 512), BF16))
            brp = Buf()
            sq_t = sbl("sqt", (128, 512))
            bsq = Buf()
            stat = sbl("stat", (1, 4 * 512))
            bst = Buf()
            stp = (sbl("stph", (1, 1024), BF16), sbl("stpl", (1, 1024), BF16))
            bstp = Buf()
            for tk in range(NT5):
                c0 = tk * 512
                for ic in range(NDCH):
                    load(ogld[:], og_src[ic * 128:(ic + 1) * 128, c0:c0 + 512], bold,
                         rbufs=(bold, bogp[ic], bog))
                    split(og_p[ic][0][:], og_p[ic][1][:], ogld[:], (bold,), bogp[ic])
                for oc in range(NDCH):
                    k = oc % 2
                    load(wsl[k][:], wo_tl_ap[oc], bws[k], rbufs=(bws[k], bwsp[k]))
                    split(wsp[k][0][:], wsp[k][1][:], wsl[k][:], (bws[k],), bwsp[k])
                    for ic in range(NDCH):
                        mm3(pbk[k][:, 0:512],
                            wsp[k][0][:, ic * 128:(ic + 1) * 128],
                            wsp[k][1][:, ic * 128:(ic + 1) * 128],
                            og_p[ic][0][:], og_p[ic][1][:],
                            bp[k], bwsp[k], bogp[ic], ic == 0, ic == NDCH - 1)
                    load(xrt[k][:], xres_ap[oc * 128:(oc + 1) * 128, c0:c0 + 512], bxrt[k],
                         rbufs=(bxrt[k],) + ((bxres,) if bxres is not None else ()))
                    vop(lambda e, oc=oc, k=k: e.tensor_tensor(r_t[oc][:], pbk[k][:, 0:512],
                                                              xrt[k][:], ALU.add),
                        reads=(bp[k], bxrt[k]), writes=(br[oc],))
                    split(rp[0][:], rp[1][:], r_t[oc][:], (br[oc],), brp)
                    mm2s(pbk[2][0:1, 0:512], ones_col[:, 0:1], rp[0][:], rp[1][:],
                         bp[2], bO, brp, oc == 0, oc == NDCH - 1)
                    sop(lambda e, oc=oc: e.activation(sq_t[:], r_t[oc][:], AF.Square),
                        reads=(br[oc],), writes=(bsq,))
                    split(rp[0][:], rp[1][:], sq_t[:], (bsq,), brp)
                    mm2s(pbk[3][0:1, 0:512], ones_col[:, 0:1], rp[0][:], rp[1][:],
                         bp[3], bO, brp, oc == 0, oc == NDCH - 1)
                vop(lambda e: e.tensor_scalar_mul(stat[0:1, 0:512], pbk[2][0:1, 0:512],
                                                  1.0 / float(D)),
                    reads=(bp[2],), writes=(bst,))
                vop(lambda e: e.tensor_scalar_mul(stat[0:1, 512:1024], pbk[3][0:1, 0:512],
                                                  1.0 / float(D)),
                    reads=(bp[3],), writes=(bst,))
                vop(lambda e: e.tensor_mul(stat[0:1, 1024:1536], stat[0:1, 0:512],
                                           stat[0:1, 0:512]),
                    reads=(bst,), writes=(bst,))
                vop(lambda e: e.tensor_sub(stat[0:1, 1024:1536], stat[0:1, 512:1024],
                                           stat[0:1, 1024:1536]),
                    reads=(bst,), writes=(bst,))
                sop(lambda e: e.activation(stat[0:1, 1024:1536], stat[0:1, 1024:1536],
                                           AF.Sqrt, bias=c_eps5[0:1, :]),
                    reads=(bst, bO), writes=(bst,))
                vop(lambda e: e.reciprocal(stat[0:1, 512:1024], stat[0:1, 1024:1536]),
                    reads=(bst,), writes=(bst,))
                vop(lambda e: e.tensor_mul(stat[0:1, 1536:2048], stat[0:1, 0:512],
                                           stat[0:1, 512:1024]),
                    reads=(bst,), writes=(bst,))
                split(stp[0][0:1, 0:512], stp[1][0:1, 0:512], stat[0:1, 512:1024],
                      (bst,), bstp)
                split(stp[0][0:1, 512:1024], stp[1][0:1, 512:1024],
                      stat[0:1, 1536:2048], (bst,), bstp)
                mm2s(pbk[4][:, 0:512], ones_row[0:1, :], stp[0][0:1, 0:512],
                     stp[1][0:1, 0:512], bp[4], bO, bstp, True, True)
                mm2s(pbk[5][:, 0:512], ones_row[0:1, :], stp[0][0:1, 512:1024],
                     stp[1][0:1, 512:1024], bp[5], bO, bstp, True, True)
                for oc in range(NDCH):
                    vop(lambda e, oc=oc: e.tensor_tensor(r_t[oc][:], r_t[oc][:],
                                                         pbk[4][:, 0:512], ALU.mult),
                        reads=(br[oc], bp[4]), writes=(br[oc],))
                    vop(lambda e, oc=oc: e.tensor_sub(r_t[oc][:], r_t[oc][:],
                                                      pbk[5][:, 0:512]),
                        reads=(br[oc], bp[5]), writes=(br[oc],))
                    store(xa_dst[oc * 128:(oc + 1) * 128, c0:c0 + 512], r_t[oc][:],
                          br[oc], wbuf=bxa)
                    if debug and dbg_key is not None:
                        store(dbg[dbg_key][oc * 128:(oc + 1) * 128, c0:c0 + 512],
                              r_t[oc][:], br[oc])

    def emit_ffn(xa_src, bxa, fwht_in, upw, gtw, csu, csg, vhh, vll, bvv, dscale_col,
                 out_stores, dbg_key):
        with contextlib.ExitStack() as outer:
            sblo = lambda n, sh, dt=F32: outer.enter_context(nc.sbuf_tensor(nm(n), list(sh), dt))
            zh = sblo("zh", (128, NDCH * L), BF16)
            zl = sblo("zl", (128, NDCH * L), BF16)
            bz_ = Buf()
            ps = contextlib.ExitStack()
            sbl = lambda n, sh, dt=F32: ps.enter_context(nc.sbuf_tensor(nm(n), list(sh), dt))
            zt = [sbl("zt", (128, 512)) for _ in range(NDCH)]
            bzt = [Buf() for _ in range(NDCH)]
            ztmp = sbl("ztmp", (128, 512))
            bztmp = Buf()
            xin = sbl("xin", (128, 512))
            bxin = Buf()
            xinp = (sbl("xinh", (128, 512), BF16), sbl("xinl", (128, 512), BF16))
            bxinp = Buf()
            for tk in range(NT5):
                c0 = tk * 512
                if fwht_in:
                    for ic in range(NDCH):
                        load(xin[:], xa_src[ic * 128:(ic + 1) * 128, c0:c0 + 512], bxin,
                             rbufs=(bxin, bxinp, bxa))
                        split(xinp[0][:], xinp[1][:], xin[:], (bxin,), bxinp)
                        mm2s(pbk[0][:, 0:512], h128su[:, ic * 128:(ic + 1) * 128],
                             xinp[0][:], xinp[1][:], bp[0], bO, bxinp, True, True)
                        sop(lambda e, ic=ic: e.copy(zt[ic][:], pbk[0][:, 0:512]),
                            reads=(bp[0],), writes=(bzt[ic],))
                    h = 1
                    while h < NDCH:
                        for blk in range(0, NDCH, 2 * h):
                            for li in range(h):
                                i0, i1 = blk + li, blk + li + h
                                vop(lambda e, a=zt[i0], b=zt[i1]: e.tensor_add(
                                    ztmp[:], a[:], b[:]),
                                    reads=(bzt[i0], bzt[i1]), writes=(bztmp,))
                                vop(lambda e, a=zt[i0], b=zt[i1]: e.tensor_sub(
                                    b[:], a[:], b[:]),
                                    reads=(bzt[i0], bzt[i1]), writes=(bzt[i1],))
                                vop(lambda e, a=zt[i0]: e.tensor_copy(a[:], ztmp[:]),
                                    reads=(bztmp,), writes=(bzt[i0],))
                        h *= 2
                    for ic in range(NDCH):
                        if debug and fwht_in and tk == 0:
                            store(dbg["zt0"][ic * 128:(ic + 1) * 128, :], zt[ic][:],
                                  bzt[ic])
                        split(zh[:, (ic * NT5 + tk) * 512:(ic * NT5 + tk) * 512 + 512],
                              zl[:, (ic * NT5 + tk) * 512:(ic * NT5 + tk) * 512 + 512],
                              zt[ic][:], (bzt[ic],), bz_)
                else:
                    for ic in range(NDCH):
                        load(xin[:], xa_src[ic * 128:(ic + 1) * 128, c0:c0 + 512], bxin,
                             rbufs=(bxin, bxa))
                        split(zh[:, (ic * NT5 + tk) * 512:(ic * NT5 + tk) * 512 + 512],
                              zl[:, (ic * NT5 + tk) * 512:(ic * NT5 + tk) * 512 + 512],
                              xin[:], (bxin,), bz_)
            ps.close()
            rec.barrier()
            ps = contextlib.ExitStack()
            sbl = lambda n, sh, dt=F32: ps.enter_context(nc.sbuf_tensor(nm(n), list(sh), dt))
            wld = [sbl("fwld", (128, D // 2)) for _ in range(4)]
            bwl = [Buf() for _ in range(4)]
            wp = [[(sbl("fwh", (128, D), BF16), sbl("fwl", (128, D), BF16)) for _ in range(2)]
                  for _ in range(2)]
            bwp = [[Buf(), Buf()], [Buf(), Buf()]]
            htile = [sbl("htile", (128, 512)) for _ in range(2)]
            bht = [Buf(), Buf()]
            hp = [(sbl("hph", (128, 512), BF16), sbl("hpl", (128, 512), BF16))
                  for _ in range(2)]
            bhp = [Buf(), Buf()]
            sil = [sbl("sil", (128, 512)) for _ in range(2)]
            bsil = [Buf(), Buf()]
            for oc in range(NICH):
                k = oc % 2
                for hf in range(2):
                    hsl = slice(hf * (D // 2), (hf + 1) * (D // 2))
                    load(wld[k][:], upw[oc, :, hsl], bwl[k], rbufs=(bwl[k], bwp[0][k]))
                    split(wp[0][k][0][:, hsl], wp[0][k][1][:, hsl], wld[k][:],
                          (bwl[k],), bwp[0][k])
                    load(wld[2 + k][:], gtw[oc, :, hsl], bwl[2 + k],
                         rbufs=(bwl[2 + k], bwp[1][k]))
                    split(wp[1][k][0][:, hsl], wp[1][k][1][:, hsl], wld[2 + k][:],
                          (bwl[2 + k],), bwp[1][k])
                for ic in range(NDCH):
                    st, sp_ = ic == 0, ic == NDCH - 1
                    for w_i, base in ((0, 0), (1, 4)):
                        wh = wp[w_i][k][0][:, ic * 128:(ic + 1) * 128]
                        wl = wp[w_i][k][1][:, ic * 128:(ic + 1) * 128]
                        bwpk = bwp[w_i][k]
                        for tk in range(NT5):
                            zc0 = zh[:, (ic * NT5 + tk) * 512:(ic * NT5 + tk) * 512 + 512]
                            zc1 = zl[:, (ic * NT5 + tk) * 512:(ic * NT5 + tk) * 512 + 512]
                            top(lambda e, wh=wh, z=zc0, p=base + tk, st=st: e.matmul(
                                pbk[p][:, 0:512], wh, z, start=st, stop=False),
                                reads=(bwpk, bz_), writes=(bp[base + tk],))
                            top(lambda e, wh=wh, z=zc1, p=base + tk: e.matmul(
                                pbk[p][:, 0:512], wh, z, start=False, stop=False),
                                reads=(bwpk, bz_), writes=(bp[base + tk],))
                        for tk in range(NT5):
                            zc0 = zh[:, (ic * NT5 + tk) * 512:(ic * NT5 + tk) * 512 + 512]
                            top(lambda e, wl=wl, z=zc0, p=base + tk, sp_=sp_: e.matmul(
                                pbk[p][:, 0:512], wl, z, start=False, stop=sp_),
                                reads=(bwpk, bz_), writes=(bp[base + tk],))
                for tk in range(NT5):
                    c0 = tk * 512
                    r = tk % 2
                    pu, pg = tk, 4 + tk
                    if csu is not None:
                        vop(lambda e, oc=oc, r=r, pg=pg: e.tensor_scalar(
                            sil[r][:], pbk[pg][:, 0:512], csg[:, oc:oc + 1], None, ALU.mult),
                            reads=(bp[pg], bO), writes=(bsil[r],))
                        sop(lambda e, r=r: e.activation(sil[r][:], sil[r][:], AF.Silu),
                            reads=(bsil[r],), writes=(bsil[r],))
                        vop(lambda e, oc=oc, r=r, pu=pu: e.scalar_tensor_tensor(
                            htile[r][:], pbk[pu][:, 0:512], csu[:, oc:oc + 1], sil[r][:],
                            ALU.mult, ALU.mult),
                            reads=(bp[pu], bsil[r], bO), writes=(bht[r],))
                    else:
                        sop(lambda e, r=r, pg=pg: e.activation(sil[r][:], pbk[pg][:, 0:512],
                                                               AF.Silu),
                            reads=(bp[pg],), writes=(bsil[r],))
                        vop(lambda e, r=r, pu=pu: e.tensor_tensor(
                            htile[r][:], pbk[pu][:, 0:512], sil[r][:], ALU.mult),
                            reads=(bp[pu], bsil[r]), writes=(bht[r],))
                    split(hp[r][0][:], hp[r][1][:], htile[r][:], (bht[r],), bhp[r])
                    store(hid_h[oc, :, c0:c0 + 512], hp[r][0][:], bhp[r], wbuf=bHID)
                    store(hid_l[oc, :, c0:c0 + 512], hp[r][1][:], bhp[r], wbuf=bHID)
            ps.close()
        rec.barrier()
        TKB = 256
        NTB = L // TKB
        with contextlib.ExitStack() as ps:
            sbl = lambda n, sh, dt=F32: ps.enter_context(nc.sbuf_tensor(nm(n), list(sh), dt))
            hpair = (sbl("hdh", (128, NICH * TKB), BF16), sbl("hdl", (128, NICH * TKB), BF16))
            bhd = [Buf() for _ in range(16)]
            vsl = [(sbl("vslh", (128, NICH * 128), BF16), sbl("vsll", (128, NICH * 128), BF16))
                   for _ in range(2)]
            bvsl = [[Buf(), Buf()], [Buf(), Buf()]]
            xrt = [sbl("dxrt", (128, TKB)) for _ in range(2)]
            bxrt = [Buf(), Buf()]
            r_t = [sbl("drt", (128, TKB)) for _ in range(NDCH)]
            br = [Buf() for _ in range(NDCH)]
            rp = (sbl("drph", (128, TKB), BF16), sbl("drpl", (128, TKB), BF16))
            brp = Buf()
            sq_t = sbl("dsqt", (128, TKB))
            bsq = Buf()
            stat = sbl("dstat", (1, 4 * TKB))
            bst = Buf()
            stp = (sbl("dstph", (1, 2 * TKB), BF16), sbl("dstpl", (1, 2 * TKB), BF16))
            bstp = Buf()
            xa_res_src, xa_res_buf = xa_src, bxa
            for tb in range(NTB):
                c0 = tb * TKB
                for grp in range(8):
                    load(hpair[0][:, grp * 8 * TKB:(grp + 1) * 8 * TKB],
                         hid_h[grp * 8:(grp + 1) * 8, :, c0:c0 + TKB].transpose([1, 0, 2]),
                         bhd[grp], rbufs=(bhd[grp], bHID))
                    load(hpair[1][:, grp * 8 * TKB:(grp + 1) * 8 * TKB],
                         hid_l[grp * 8:(grp + 1) * 8, :, c0:c0 + TKB].transpose([1, 0, 2]),
                         bhd[8 + grp], rbufs=(bhd[8 + grp], bHID)) 
                for oc in range(NDCH):
                    k = oc % 2
                    load(vsl[k][0][:], vhh[oc, :, :, :].rearrange("p a b -> p (a b)"),
                         bvsl[k][0], rbufs=(bvsl[k][0], bvv))
                    load(vsl[k][1][:], vll[oc, :, :, :].rearrange("p a b -> p (a b)"),
                         bvsl[k][1], rbufs=(bvsl[k][1], bvv))
                    for ic in range(NICH):
                        hc = (hpair[0][:, ic * TKB:(ic + 1) * TKB],
                              hpair[1][:, ic * TKB:(ic + 1) * TKB])
                        bh1 = bhd[ic // 8]
                        bh2 = bhd[8 + ic // 8]
                        top(lambda e, k=k, ic=ic, hc=hc, oc=oc: e.matmul(
                            pbk[k][:, 0:TKB], vsl[k][0][:, ic * 128:(ic + 1) * 128],
                            hc[0], start=ic == 0, stop=False),
                            reads=(bvsl[k][0], bh1), writes=(bp[k],))
                        top(lambda e, k=k, ic=ic, hc=hc: e.matmul(
                            pbk[k][:, 0:TKB], vsl[k][0][:, ic * 128:(ic + 1) * 128],
                            hc[1], start=False, stop=False),
                            reads=(bvsl[k][0], bh2), writes=(bp[k],))
                        top(lambda e, k=k, ic=ic, hc=hc: e.matmul(
                            pbk[k][:, 0:TKB], vsl[k][1][:, ic * 128:(ic + 1) * 128],
                            hc[0], start=False, stop=ic == NICH - 1),
                            reads=(bvsl[k][1], bh1), writes=(bp[k],))
                    load(xrt[k][:], xa_res_src[oc * 128:(oc + 1) * 128, c0:c0 + TKB],
                         bxrt[k], rbufs=(bxrt[k], xa_res_buf))
                    if dscale_col is not None:
                        vop(lambda e, oc=oc, k=k: e.scalar_tensor_tensor(
                            r_t[oc][:], pbk[k][:, 0:TKB], dscale_col[:, oc:oc + 1],
                            xrt[k][:], ALU.mult, ALU.add),
                            reads=(bp[k], bxrt[k], bO), writes=(br[oc],))
                    else:
                        vop(lambda e, oc=oc, k=k: e.scalar_tensor_tensor(
                            r_t[oc][:], pbk[k][:, 0:TKB],
                            1.0 / float(np.sqrt(8192.0)), xrt[k][:], ALU.mult, ALU.add),
                            reads=(bp[k], bxrt[k]), writes=(br[oc],))
                    split(rp[0][:], rp[1][:], r_t[oc][:], (br[oc],), brp)
                    mm2s(pbk[2][0:1, 0:TKB], ones_col[:, 0:1], rp[0][:], rp[1][:],
                         bp[2], bO, brp, oc == 0, oc == NDCH - 1)
                    sop(lambda e, oc=oc: e.activation(sq_t[:], r_t[oc][:], AF.Square),
                        reads=(br[oc],), writes=(bsq,))
                    split(rp[0][:], rp[1][:], sq_t[:], (bsq,), brp)
                    mm2s(pbk[3][0:1, 0:TKB], ones_col[:, 0:1], rp[0][:], rp[1][:],
                         bp[3], bO, brp, oc == 0, oc == NDCH - 1)
                vop(lambda e: e.tensor_scalar_mul(stat[0:1, 0:TKB], pbk[2][0:1, 0:TKB],
                                                  1.0 / float(D)),
                    reads=(bp[2],), writes=(bst,))
                vop(lambda e: e.tensor_scalar_mul(stat[0:1, TKB:2 * TKB],
                                                  pbk[3][0:1, 0:TKB], 1.0 / float(D)),
                    reads=(bp[3],), writes=(bst,))
                vop(lambda e: e.tensor_mul(stat[0:1, 2 * TKB:3 * TKB], stat[0:1, 0:TKB],
                                           stat[0:1, 0:TKB]),
                    reads=(bst,), writes=(bst,))
                vop(lambda e: e.tensor_sub(stat[0:1, 2 * TKB:3 * TKB],
                                           stat[0:1, TKB:2 * TKB],
                                           stat[0:1, 2 * TKB:3 * TKB]),
                    reads=(bst,), writes=(bst,))
                sop(lambda e: e.activation(stat[0:1, 2 * TKB:3 * TKB],
                                           stat[0:1, 2 * TKB:3 * TKB], AF.Sqrt,
                                           bias=c_eps5[0:1, :]),
                    reads=(bst, bO), writes=(bst,))
                vop(lambda e: e.reciprocal(stat[0:1, TKB:2 * TKB],
                                           stat[0:1, 2 * TKB:3 * TKB]),
                    reads=(bst,), writes=(bst,))
                vop(lambda e: e.tensor_mul(stat[0:1, 3 * TKB:4 * TKB], stat[0:1, 0:TKB],
                                           stat[0:1, TKB:2 * TKB]),
                    reads=(bst,), writes=(bst,))
                split(stp[0][0:1, 0:TKB], stp[1][0:1, 0:TKB], stat[0:1, TKB:2 * TKB],
                      (bst,), bstp)
                split(stp[0][0:1, TKB:2 * TKB], stp[1][0:1, TKB:2 * TKB],
                      stat[0:1, 3 * TKB:4 * TKB], (bst,), bstp)
                mm2s(pbk[4][:, 0:TKB], ones_row[0:1, :], stp[0][0:1, 0:TKB],
                     stp[1][0:1, 0:TKB], bp[4], bO, bstp, True, True)
                mm2s(pbk[5][:, 0:TKB], ones_row[0:1, :], stp[0][0:1, TKB:2 * TKB],
                     stp[1][0:1, TKB:2 * TKB], bp[5], bO, bstp, True, True)
                for oc in range(NDCH):
                    vop(lambda e, oc=oc: e.tensor_tensor(r_t[oc][:], r_t[oc][:],
                                                         pbk[4][:, 0:TKB], ALU.mult),
                        reads=(br[oc], bp[4]), writes=(br[oc],))
                    vop(lambda e, oc=oc: e.tensor_sub(r_t[oc][:], r_t[oc][:],
                                                      pbk[5][:, 0:TKB]),
                        reads=(br[oc], bp[5]), writes=(br[oc],))
                    for dst_ap_fn, wb in out_stores:
                        store(dst_ap_fn(oc, c0, TKB), r_t[oc][:], br[oc], wbuf=wb)
                    if debug and dbg_key is not None:
                        store(dbg[dbg_key][oc * 128:(oc + 1) * 128, c0:c0 + TKB],
                              r_t[oc][:], br[oc])

    def emit_p2():
        NTS = T // 512
        with contextlib.ExitStack() as ps:
            sbl = lambda n, sh, dt=F32: ps.enter_context(nc.sbuf_tensor(nm(n), list(sh), dt))
            wsb = {}
            bW = Buf()
            ldw = sbl("p2wld", (128, 256))
            bldw = Buf()
            for pname in ("q", "k", "v"):
                wsb[pname] = (sbl("w2h_", (128, NDCH * 128), BF16),
                              sbl("w2l_", (128, NDCH * 128), BF16))

            def p2_load_weights(hl):
                for pname, src in (("q", wq1), ("k", wk1), ("v", wv1)):
                    for ic in range(NDCH):
                        load(ldw[:, 0:128], src[ic * 128:(ic + 1) * 128,
                                                hl * 128:(hl + 1) * 128], bldw,
                             rbufs=(bldw, bW))
                        hi_t, lo_t = wsb[pname]
                        split(hi_t[:, ic * 128:(ic + 1) * 128],
                              lo_t[:, ic * 128:(ic + 1) * 128],
                              ldw[:, 0:128], (bldw,), bW)
            xx = sbl("p2xld", (128, 512))
            bxl = Buf()
            xpp = (sbl("p2xh", (128, 512), BF16), sbl("p2xl", (128, 512), BF16))
            bxp = Buf()
            rope = sbl("p2rope", (128, 2 * 512))
            brope = Buf()
            rop_t = sbl("p2ropt", (128, 512))
            brt = Buf()
            q32 = sbl("p2q", (128, T))
            bq32 = Buf()
            kpr = (sbl("p2kh", (128, T), BF16), sbl("p2kl", (128, T), BF16))
            bkp = Buf()
            vtm = (sbl("p2vth", (128, T), BF16), sbl("p2vtl", (128, T), BF16))
            bvt = Buf()
            vfp = (sbl("p2vfh", (128, 512), BF16), sbl("p2vfl", (128, 512), BF16))
            bvf = Buf()
            qbp = [(sbl("p2qbh", (128, 512), BF16), sbl("p2qbl", (128, 512), BF16))
                   for _ in range(2)]
            bqb = [Buf(), Buf()]
            w1t = sbl("p2w1", (128, 512))
            bw1 = Buf()
            w2t = [sbl("p2w2", (128, 512)) for _ in range(4)]
            bw2_ = [Buf() for _ in range(4)]
            w2pp = [(sbl("p2w2h", (128, 512), BF16), sbl("p2w2l", (128, 512), BF16))
                    for _ in range(4)]
            bw2p = [Buf() for _ in range(4)]
            wq1t = [sbl("p2w1r", (128, 512)) for _ in range(4)]
            bwq1 = [Buf() for _ in range(4)]
            den_s = [sbl("p2dens", (1, 512)) for _ in range(2)]
            bden_s = [Buf(), Buf()]
            den = sbl("p2den", (1, 512))
            den2 = sbl("p2den2", (1, 512))
            bden = Buf()
            denp = (sbl("p2dh", (1, 512), BF16), sbl("p2dl", (1, 512), BF16))
            bdenp = Buf()
            dbc = sbl("p2dbc", (128, 512))
            bdbc = Buf()
            ot = sbl("p2ot", (128, 512))
            bot = Buf()
            for b in range(B):
                for hl in range(2):
                    p2_load_weights(hl)
                    for ts in range(NTS):
                        t0 = ts * 512
                        g0 = b * T + t0
                        blk = g0 // L
                        co = g0 % L
                        load(rope[:, 0:512], cosT[:, t0:t0 + 512], brope, rbufs=(brope,))
                        load(rope[:, 512:1024], sinT[:, t0:t0 + 512], brope, rbufs=(brope,))
                        for ic in range(NDCH):
                            load(xx[:], ag2_out[blk * D + ic * 128:blk * D + (ic + 1) * 128,
                                                co:co + 512], bxl,
                                 rbufs=(bxl, bxp, bAG2))
                            split(xpp[0][:], xpp[1][:], xx[:], (bxl,), bxp)
                            for pj, pbi in (("q", 0), ("k", 1), ("v", 2)):
                                w = wsb[pj]
                                mm3(pbk[pbi][:, 0:512],
                                    w[0][:, ic * 128:(ic + 1) * 128],
                                    w[1][:, ic * 128:(ic + 1) * 128],
                                    xpp[0][:], xpp[1][:],
                                    bp[pbi], bW, bxp, ic == 0, ic == NDCH - 1)
                        for pbi in (0, 1):
                            vop(lambda e, i=pbi: e.tensor_tensor(
                                rop_t[:], pbk[i][:, 0:512], rope[:, 0:512], ALU.mult),
                                reads=(bp[pbi], brope), writes=(brt,))
                            vop(lambda e, i=pbi: e.tensor_tensor(
                                pbk[6][:, 0:512], pbk[i][:, 0:512], rope[:, 512:1024],
                                ALU.mult),
                                reads=(bp[pbi], brope), writes=(bp[6],))
                            if pbi == 0:
                                vop(lambda e, t0=t0: e.tensor_sub(
                                    q32[0:64, t0:t0 + 512], rop_t[0:64, :],
                                    pbk[6][64:128, 0:512]),
                                    reads=(brt, bp[6]), writes=(bq32,))
                                vop(lambda e, t0=t0: e.tensor_add(
                                    q32[64:128, t0:t0 + 512], rop_t[64:128, :],
                                    pbk[6][0:64, 0:512]),
                                    reads=(brt, bp[6]), writes=(bq32,))
                            else:
                                vop(lambda e: e.tensor_sub(
                                    w1t[0:64, :], rop_t[0:64, :], pbk[6][64:128, 0:512]),
                                    reads=(brt, bp[6]), writes=(bw1,))
                                vop(lambda e: e.tensor_add(
                                    w1t[64:128, :], rop_t[64:128, :], pbk[6][0:64, 0:512]),
                                    reads=(brt, bp[6]), writes=(bw1,))
                                split(kpr[0][:, t0:t0 + 512], kpr[1][:, t0:t0 + 512],
                                      w1t[:], (bw1,), bkp)
                        split(vfp[0][:], vfp[1][:], pbk[2][:, 0:512], (bp[2],), bvf)
                        for q4 in range(4):
                            for ip in range(2):
                                top(lambda e, ip=ip, q4=q4: e.transpose(
                                    pbk[5][:, 0:64].bitcast(BF16),
                                    vfp[ip][:, q4 * 128:(q4 + 1) * 128], ident_b[:]),
                                    reads=(bvf, bO), writes=(bp[5],))
                                vop(lambda e, ip=ip, q4=q4, t0=t0: e.tensor_copy(
                                    vtm[ip][:, t0 + q4 * 128:t0 + (q4 + 1) * 128],
                                    pbk[5][:, 0:64].bitcast(BF16)),
                                    reads=(bp[5],), writes=(bvt,))
                    for tbp in range(NTS // 2):
                        tbs = (2 * tbp, 2 * tbp + 1)
                        for a in range(2):
                            split(qbp[a][0][:], qbp[a][1][:],
                                  q32[:, tbs[a] * 512:tbs[a] * 512 + 512], (bq32,), bqb[a])
                        njt = 4 * tbs[1] + 4
                        for j in range(njt):
                            acts = [a for a in range(2) if j < 4 * tbs[a] + 4]
                            for a in acts:
                                pbs = 4 + 2 * (j % 2) + a
                                r = 2 * (j % 2) + a
                                tbv = tbs[a]
                                first_j = j == 0
                                last_j = j == 4 * tbv + 3
                                top(lambda e, p=pbs, j=j, a=a: e.matmul(
                                    pbk[p][:, 0:512], kpr[0][:, j * 128:(j + 1) * 128],
                                    qbp[a][0][:], start=True, stop=False),
                                    reads=(bkp, bqb[a]), writes=(bp[pbs],))
                                top(lambda e, p=pbs, j=j, a=a: e.matmul(
                                    pbk[p][:, 0:512], kpr[0][:, j * 128:(j + 1) * 128],
                                    qbp[a][1][:], start=False, stop=False),
                                    reads=(bkp, bqb[a]), writes=(bp[pbs],))
                                top(lambda e, p=pbs, j=j, a=a: e.matmul(
                                    pbk[p][:, 0:512], kpr[1][:, j * 128:(j + 1) * 128],
                                    qbp[a][0][:], start=False, stop=True),
                                    reads=(bkp, bqb[a]), writes=(bp[pbs],))
                                vdiag = j - 4 * tbv
                                if vdiag >= 0:
                                    vop(lambda e, p=pbs, v=vdiag, r=r: e.tensor_tensor(
                                        wq1t[r][:], pbk[p][:, 0:512],
                                        masks4[:, v * 512:(v + 1) * 512], ALU.mult),
                                        reads=(bp[pbs], bO), writes=(bwq1[r],))
                                    sop(lambda e, r=r: e.activation(wq1t[r][:], wq1t[r][:],
                                                                    AF.Square),
                                        reads=(bwq1[r],), writes=(bwq1[r],))
                                else:
                                    sop(lambda e, p=pbs, r=r: e.activation(
                                        wq1t[r][:], pbk[p][:, 0:512], AF.Square),
                                        reads=(bp[pbs],), writes=(bwq1[r],))
                                vop(lambda e, r=r: e.tensor_mul(w2t[r][:], wq1t[r][:],
                                                                wq1t[r][:]),
                                    reads=(bwq1[r],), writes=(bw2_[r],))
                                split(w2pp[r][0][:], w2pp[r][1][:], w2t[r][:], (bw2_[r],),
                                      bw2p[r])
                                top(lambda e, j=j, r=r, a=a: e.matmul(
                                    pbk[2 + a][:, 0:512], vtm[0][:, j * 128:(j + 1) * 128],
                                    w2pp[r][0][:], start=j == 0, stop=False),
                                    reads=(bvt, bw2p[r]), writes=(bp[2 + a],))
                                top(lambda e, j=j, r=r, a=a: e.matmul(
                                    pbk[2 + a][:, 0:512], vtm[0][:, j * 128:(j + 1) * 128],
                                    w2pp[r][1][:], start=False, stop=False),
                                    reads=(bvt, bw2p[r]), writes=(bp[2 + a],))
                                top(lambda e, j=j, r=r, a=a, lj=last_j: e.matmul(
                                    pbk[2 + a][:, 0:512], vtm[1][:, j * 128:(j + 1) * 128],
                                    w2pp[r][0][:], start=False, stop=lj),
                                    reads=(bvt, bw2p[r]), writes=(bp[2 + a],))
                                mm2s(pbk[a][0:1, 0:512], ones_col[:, 0:1],
                                     w2pp[r][0][:], w2pp[r][1][:], bp[a], bO, bw2p[r],
                                     j == 0, last_j)
                        for a in range(2):
                            tbv = tbs[a]
                            t0 = tbv * 512
                            vop(lambda e, a=a: e.tensor_scalar_max(
                                den_s[a][:], pbk[a][0:1, 0:512], 1e-6),
                                reads=(bp[a],), writes=(bden_s[a],))
                            vop(lambda e, a=a: e.reciprocal(den2[:], den_s[a][:]),
                                reads=(bden_s[a],), writes=(bden,))
                            split(denp[0][:], denp[1][:], den2[:], (bden,), bdenp)
                            mm2s(pbk[4][:, 0:512], ones_row[0:1, :], denp[0][:], denp[1][:],
                                 bp[4], bO, bdenp, True, True)
                            sop(lambda e: e.copy(dbc[:], pbk[4][:, 0:512]),
                                reads=(bp[4],), writes=(bdbc,))
                            vop(lambda e, a=a: e.tensor_tensor(ot[:], pbk[2 + a][:, 0:512],
                                                               dbc[:], ALU.mult),
                                reads=(bp[2 + a], bdbc), writes=(bot,))
                            g0 = b * T + t0
                            blk = g0 // L
                            co = g0 % L
                            store(a2a3_in[blk * 256 + hl * 128:blk * 256 + (hl + 1) * 128,
                                          co:co + 512], ot[:], bot, wbuf=bIN3)
                            if debug:
                                store(dbg["o1"][hl * 128:(hl + 1) * 128, g0:g0 + 512],
                                      ot[:], bot)

    emit_consts()
    emit_vprep(wdn_t, vdn_h, vdn_l, True, bVDN)
    rec.barrier()
    emit_vprep(w2_t, v2_h, v2_l, False, bV2)
    rec.barrier()
    emit_p0()
    rec.barrier()
    emit_collective("AllToAll", a2a1_in, a2a1_out, bIN1, bA2A1)
    emit_proj_ln(a2a1_out, bA2A1, wo0_tl, x0loc, None, x1a_s, bX1A, "x1a")
    rec.barrier()
    emit_ffn(x1a_s, bX1A, True, wup_tl, wgt_tl, cs_up, cs_gt, vdn_h, vdn_l, bVDN,
             cs_dn,
             [(lambda oc, c0, tkb: ag2_in[oc * 128:(oc + 1) * 128, c0:c0 + tkb], bIN2),
              (lambda oc, c0, tkb: x1loc_s[oc * 128:(oc + 1) * 128, c0:c0 + tkb], bX1L)],
             "x1")
    rec.barrier()
    emit_collective("AllGather", ag2_in, ag2_out, bIN2, bAG2)
    emit_p2()
    rec.barrier()
    emit_collective("AllToAll", a2a3_in, a2a3_out, bIN3, bA2A3)
    emit_proj_ln(a2a3_out, bA2A3, wo1_tl, x1loc_s, bX1L, x2a_s, bX2A, "x2a")
    rec.barrier()
    emit_ffn(x2a_s, bX2A, False, w3_tl, w1_tl, None, None, v2_h, v2_l, bV2, None,
             [(lambda oc, c0, tkb: y[oc * 128:(oc + 1) * 128, c0:c0 + tkb], bY)],
             None)
    if debug:
        rec.op("scalar", lambda e: e.dma_start(out=dbg["vdnh"][:, :, :, :], in_=vdn_h[:, :, :, :]),
               reads=(bVDN,), writes=(), queue="dA")
        rec.op("scalar", lambda e: e.dma_start(out=dbg["vdnl"][:, :, :, :], in_=vdn_l[:, :, :, :]),
               reads=(bVDN,), writes=(), queue="dA")
        rec.op("scalar", lambda e: e.dma_start(out=dbg["hidh"][:, :, :], in_=hid_h[:, :, :]),
               reads=(bHID,), writes=(), queue="dA")
        rec.op("scalar", lambda e: e.dma_start(out=dbg["hidl"][:, :, :], in_=hid_l[:, :, :]),
               reads=(bHID,), writes=(), queue="dA")
    rec.op("vector", lambda e: e.engine_nop(), reads=(bY,), writes=())

    rec.finalize()
    with stk, nc.Block() as block, contextlib.ExitStack() as semstk:
        sems = {s: semstk.enter_context(nc.semaphore(f"sem_{s}"))
                for s in ENGINES + QUEUES}
        for eng in ENGINES:
            getattr(block, eng)(lambda proxy, eng=eng: rec.emit(eng, proxy, sems))
    return nc



def _hadamard(n):
    h = np.array([[1.0]], dtype=np.float32)
    while h.shape[0] < n:
        h = np.block([[h, h], [h, -h]])
    return h.astype(np.float32)


def _prep_inputs(inputs, B=2, T=8192):
    D = 2048
    INTER = 8192
    L = B * T // NCORES
    x0 = np.asarray(inputs["L0_attn_hidden_states"], dtype=np.float32)
    x0f = np.ascontiguousarray(x0.reshape(B * T, D).T)

    def tl(wt, nich):
        din_, dout = wt.shape
        return np.ascontiguousarray(
            wt.reshape(din_ // 128, 128, dout // 128, 128).transpose(2, 1, 0, 3)
            .reshape(dout // 128, 128, din_))

    WT = {k: np.asarray(v, dtype=np.float32) for k, v in inputs.items()}
    wo0_tl = tl(np.ascontiguousarray(WT["L0_attn_Wo"].T), 16)
    wup_tl = tl(np.ascontiguousarray(WT["L0_ffn_W_up"].T), 64)
    wgt_tl = tl(np.ascontiguousarray(WT["L0_ffn_W_gate"].T), 64)
    wdn_t = np.ascontiguousarray(WT["L0_ffn_W_down"].T)
    wo1_tl = tl(np.ascontiguousarray(WT["L1_attn_Wo"].T), 16)
    w1_tl = tl(np.ascontiguousarray(WT["L1_ffn_W1"].T), 64)
    w3_tl = tl(np.ascontiguousarray(WT["L1_ffn_W3"].T), 64)
    w2_t = np.ascontiguousarray(WT["L1_ffn_W2"].T)

    inv = (10000.0 ** (-(np.arange(64, dtype=np.float64) / 64.0)))
    ang = inv[:, None] * np.arange(T, dtype=np.float64)[None, :]
    cos1 = np.cos(ang).astype(np.float32)
    sin1 = np.sin(ang).astype(np.float32)
    cosT = np.concatenate([cos1, cos1], axis=0)
    sinT = np.concatenate([sin1, sin1], axis=0)
    scale = np.float32(1.0 / np.sqrt(128.0))
    cosqT = (cosT * scale).astype(np.float32)
    sinqT = (sinT * scale).astype(np.float32)

    ident = np.eye(128, dtype=np.float32)
    jj, tt = np.meshgrid(np.arange(128), np.arange(128), indexing="ij")
    hmaskT = (jj <= tt).astype(np.float32)
    m4 = np.zeros((128, 4 * 512), dtype=np.float32)
    for v in range(4):
        j0, t0g = np.meshgrid(np.arange(128), np.arange(512), indexing="ij")
        m4[:, v * 512:(v + 1) * 512] = (v * 128 + j0 <= t0g).astype(np.float32)
    h128 = _hadamard(128)

    su_ug_fm = np.ascontiguousarray(WT["L0_ffn_SU_ug"].reshape(16, 128).T)
    su_dp_fm = np.ascontiguousarray(WT["L0_ffn_SU_dp"].reshape(64, 128).T)
    wsc_up_fm = np.ascontiguousarray(WT["L0_ffn_Wscale_ug"][:INTER].reshape(64, 128).T)
    wsc_gt_fm = np.ascontiguousarray(WT["L0_ffn_Wscale_ug"][INTER:].reshape(64, 128).T)
    wsc_dn_fm = np.ascontiguousarray(WT["L0_ffn_Wscale_dp"].reshape(16, 128).T)

    wq0t = np.ascontiguousarray(WT["L0_attn_Wq"].T)
    wk0t = np.ascontiguousarray(WT["L0_attn_Wk"].T)
    wv0t = np.ascontiguousarray(WT["L0_attn_Wv"].T)
    wg0t = np.ascontiguousarray(WT["L0_attn_Wg"].T)
    wq1t = np.ascontiguousarray(WT["L1_attn_Wq"].T)
    wk1t = np.ascontiguousarray(WT["L1_attn_Wk"].T)
    wv1t = np.ascontiguousarray(WT["L1_attn_Wv"].T)
    ws1t = np.ascontiguousarray(WT["L0_attn_Ws1"].T)

    in_maps = []
    for c in range(NCORES):
        hs = slice(c * 256, (c + 1) * 256)
        m = {
            "x0f": x0f,
            "x0loc": np.ascontiguousarray(x0f[:, c * L:(c + 1) * L]),
            "wq0": np.ascontiguousarray(wq0t[:, hs]),
            "wk0": np.ascontiguousarray(wk0t[:, hs]),
            "wv0": np.ascontiguousarray(wv0t[:, hs]),
            "wg0": np.ascontiguousarray(wg0t[:, hs]),
            "ws1t": ws1t,
            "ws2t": np.ascontiguousarray(WT["L0_attn_Ws2"][c * 128:(c + 1) * 128].T),
            "bs2c": np.ascontiguousarray(WT["L0_attn_bs2"][c * 128:(c + 1) * 128]
                                         .reshape(1, 128)),
            "gnw": np.ascontiguousarray(WT["L0_attn_g_norm_weight"].reshape(128, 1)),
            "wo0_tl": wo0_tl,
            "wup_tl": wup_tl, "wgt_tl": wgt_tl, "wdn_t": wdn_t,
            "su_ug_fm": su_ug_fm, "su_dp_fm": su_dp_fm,
            "wsc_up_fm": wsc_up_fm, "wsc_gt_fm": wsc_gt_fm, "wsc_dn_fm": wsc_dn_fm,
            "wq1": np.ascontiguousarray(wq1t[:, hs]),
            "wk1": np.ascontiguousarray(wk1t[:, hs]),
            "wv1": np.ascontiguousarray(wv1t[:, hs]),
            "wo1_tl": wo1_tl,
            "w1_tl": w1_tl, "w3_tl": w3_tl, "w2_t": w2_t,
            "cosT": cosT, "sinT": sinT, "cosqT": cosqT, "sinqT": sinqT,
            "ident_in": ident, "hmaskT_in": hmaskT, "masks4_in": m4,
            "h128_in": h128,
        }
        in_maps.append(m)
    return in_maps


def kernel(**inputs):
    B, T, D = 2, 8192, 2048
    L = B * T // NCORES
    in_maps = _prep_inputs(inputs, B, T)
    nc = build_nc(B, T, debug=False, paranoid=True)
    res = run_bass_kernel_spmd(nc, in_maps, core_ids=list(range(NCORES)))
    parts = [res.results[c]["y"] for c in range(NCORES)]
    full = np.concatenate(parts, axis=1)
    out = np.ascontiguousarray(full.T).reshape(B, T, D).astype(np.float32)
    return out


# version 07 (final)
# speedup vs baseline: 1.0694x; 1.0694x over previous
import contextlib
import numpy as np

import concourse.bass as bass
import concourse.mybir as mybir
from concourse.bass_utils import run_bass_kernel_spmd
from concourse.aot_env import allow_ici_all_to_all

F32 = mybir.dt.float32
BF16 = mybir.dt.bfloat16
AF = mybir.ActivationFunctionType
ALU = mybir.AluOpType
AX = mybir.AxisListType

NCORES = 8
ENGINES = ("tensor", "vector", "scalar", "gpsimd", "sync")
QR = 12
_QBASE = ("dS", "dA", "dG", "cc")
QUEUES = tuple(f"{q}#{i}" for q in _QBASE for i in range(QR))
_QNEXT = {q: 0 for q in _QBASE}


def _qsub(queue):
    i = _QNEXT[queue]
    _QNEXT[queue] = (i + 1) % QR
    return f"{queue}#{i}"


class Buf:
    __slots__ = ("last_w", "readers")

    def __init__(self):
        self.last_w = None
        self.readers = {}


class Rec:

    def __init__(self, full_edges=False):
        self.full_edges = full_edges
        self.ops = []
        self.seq = {s: 0 for s in ENGINES + QUEUES}
        self.need = {s: set() for s in ENGINES + QUEUES}
        self.bar = {s: 0 for s in ENGINES + QUEUES}

    def barrier(self):
        self.bar = dict(self.seq)

    def op(self, eng, fn, reads=(), writes=(), queue=None):
        stream = _qsub(queue) if queue is not None else eng
        waits = {}
        for s, q in self.bar.items():
            if s != stream and q > 0:
                waits[s] = q
        if queue is not None and self.seq[stream] > 0:
            waits[stream] = self.seq[stream]
        fe = self.full_edges
        for b in reads:
            lw = b.last_w
            if lw is not None and (fe or lw[0] != stream) and waits.get(lw[0], -1) < lw[1]:
                waits[lw[0]] = lw[1]
        for b in writes:
            lw = b.last_w
            if lw is not None and (fe or lw[0] != stream) and waits.get(lw[0], -1) < lw[1]:
                waits[lw[0]] = lw[1]
            for s, q in b.readers.items():
                if (fe or s != stream) and waits.get(s, -1) < q:
                    waits[s] = q
        self.seq[stream] += 1
        myseq = self.seq[stream]
        for b in reads:
            if b.readers.get(stream, -1) < myseq:
                b.readers[stream] = myseq
        for b in writes:
            b.last_w = (stream, myseq)
            b.readers = {}
        for s, q in waits.items():
            self.need[s].add(q)
        self.ops.append((eng, fn, waits, stream, myseq))

    def finalize(self):
        self.cnt = {}
        self.inc = {}
        for s in ENGINES + QUEUES:
            need = self.need[s]
            always = s not in ENGINES
            cnt = [0] * (self.seq[s] + 1)
            inc = [False] * (self.seq[s] + 1)
            c = 0
            for q in range(1, self.seq[s] + 1):
                if always or q in need:
                    c += 1
                    inc[q] = True
                cnt[q] = c
            self.cnt[s] = cnt
            self.inc[s] = inc

    def emit(self, eng_name, proxy, sems):
        wm = {s: 0 for s in ENGINES + QUEUES}
        for eng, fn, waits, stream, myseq in self.ops:
            if eng != eng_name:
                continue
            for s in sorted(waits):
                tgt = self.cnt[s][waits[s]]
                if tgt > wm[s]:
                    step = 16 if s.startswith("d") else 1
                    proxy.wait_ge(sems[s], tgt * step)
                    wm[s] = tgt
            inst = fn(proxy)
            if self.inc[stream][myseq]:
                step = 16 if stream.startswith("d") else 1
                inst.then_inc(sems[stream], step)


def build_nc(B=2, T=8192, debug=False, paranoid=False):
    D = 2048
    INTER = 8192
    L = B * T // NCORES
    SEG = 512
    CH = 128
    NDCH = D // 128
    NICH = INTER // 128
    NT5 = L // 512
    assert T % SEG == 0 and L % 512 == 0

    nc = bass.Bass("TRN2", target_bir_lowering=False,
                   detect_race_conditions=paranoid)

    def din(name, shape, dt=F32):
        return nc.dram_tensor(name, list(shape), dt, kind="ExternalInput").ap()

    x0f = din("x0f", (D, B * T))
    x0loc = din("x0loc", (D, L))
    wq0 = din("wq0", (D, 256)); wk0 = din("wk0", (D, 256))
    wv0 = din("wv0", (D, 256)); wg0 = din("wg0", (D, 256))
    ws1t = din("ws1t", (D, 16)); ws2t = din("ws2t", (16, 128))
    bs2c = din("bs2c", (1, 128)); gnw = din("gnw", (128, 1))
    wo0_tl = din("wo0_tl", (NDCH, 128, D))
    wup_tl = din("wup_tl", (NICH, 128, D))
    wgt_tl = din("wgt_tl", (NICH, 128, D))
    wdn_t = din("wdn_t", (INTER, D))
    su_ug_fm = din("su_ug_fm", (128, 16))
    su_dp_fm = din("su_dp_fm", (128, 64))
    wsc_up_fm = din("wsc_up_fm", (128, 64))
    wsc_gt_fm = din("wsc_gt_fm", (128, 64))
    wsc_dn_fm = din("wsc_dn_fm", (128, 16))
    wq1 = din("wq1", (D, 256)); wk1 = din("wk1", (D, 256)); wv1 = din("wv1", (D, 256))
    wo1_tl = din("wo1_tl", (NDCH, 128, D))
    w1_tl = din("w1_tl", (NICH, 128, D))
    w3_tl = din("w3_tl", (NICH, 128, D))
    w2_t = din("w2_t", (INTER, D))
    cosT = din("cosT", (128, T)); sinT = din("sinT", (128, T))
    cosqT = din("cosqT", (128, T)); sinqT = din("sinqT", (128, T))
    ident_in = din("ident_in", (128, 128))
    hmaskT_in = din("hmaskT_in", (128, 128))
    masks4_in = din("masks4_in", (128, 4 * 512))
    h128_in = din("h128_in", (128, 128))

    y = nc.dram_tensor("y", [D, L], F32, kind="ExternalOutput").ap()

    def dscr(name, shape, dt=F32):
        return nc.dram_tensor(name, list(shape), dt).ap()

    a2a1_in = dscr("a2a1_in", (NCORES * 256, L))
    a2a1_out = dscr("a2a1_out", (NCORES * 256, L))
    ag2_in = dscr("ag2_in", (D, L))
    ag2_out = dscr("ag2_out", (NCORES * D, L))
    a2a3_in = dscr("a2a3_in", (NCORES * 256, L))
    a2a3_out = dscr("a2a3_out", (NCORES * 256, L))
    x1a_s = dscr("x1a_s", (D, L))
    x1loc_s = dscr("x1loc_s", (D, L))
    x2a_s = dscr("x2a_s", (D, L))
    hid_h = dscr("hid_h", (NICH, 128, L), BF16)
    hid_l = dscr("hid_l", (NICH, 128, L), BF16)
    vdn_h = dscr("vdn_h", (16, 128, NICH, 128), BF16)
    vdn_l = dscr("vdn_l", (16, 128, NICH, 128), BF16)
    v2_h = dscr("v2_h", (16, 128, NICH, 128), BF16)
    v2_l = dscr("v2_l", (16, 128, NICH, 128), BF16)

    dbg = {}
    if debug:
        dbg["og"] = nc.dram_tensor("dbg_og", [256, B * T], F32, kind="ExternalOutput").ap()
        dbg["x1a"] = nc.dram_tensor("dbg_x1a", [D, L], F32, kind="ExternalOutput").ap()
        dbg["x1"] = nc.dram_tensor("dbg_x1", [D, L], F32, kind="ExternalOutput").ap()
        dbg["o1"] = nc.dram_tensor("dbg_o1", [256, B * T], F32, kind="ExternalOutput").ap()
        dbg["x2a"] = nc.dram_tensor("dbg_x2a", [D, L], F32, kind="ExternalOutput").ap()
        dbg["vdnh"] = nc.dram_tensor("dbg_vdnh", [16, 128, NICH, 128], BF16, kind="ExternalOutput").ap()
        dbg["vdnl"] = nc.dram_tensor("dbg_vdnl", [16, 128, NICH, 128], BF16, kind="ExternalOutput").ap()
        dbg["hidh"] = nc.dram_tensor("dbg_hidh", [NICH, 128, L], BF16, kind="ExternalOutput").ap()
        dbg["hidl"] = nc.dram_tensor("dbg_hidl", [NICH, 128, L], BF16, kind="ExternalOutput").ap()
        dbg["q0"] = nc.dram_tensor("dbg_q0", [128, 512], F32, kind="ExternalOutput").ap()
        dbg["k0"] = nc.dram_tensor("dbg_k0", [128, 512], F32, kind="ExternalOutput").ap()
        dbg["es0"] = nc.dram_tensor("dbg_es0", [128, 64], F32, kind="ExternalOutput").ap()
        dbg["zz0"] = nc.dram_tensor("dbg_zz0", [128, 64], F32, kind="ExternalOutput").ap()
        dbg["ok0"] = nc.dram_tensor("dbg_ok0", [128, 64], F32, kind="ExternalOutput").ap()
        dbg["qv0"] = nc.dram_tensor("dbg_qv0", [128, 64], F32, kind="ExternalOutput").ap()
        dbg["oo0"] = nc.dram_tensor("dbg_oo0", [128, 512], F32, kind="ExternalOutput").ap()
        dbg["S0"] = nc.dram_tensor("dbg_S0", [128, 64], F32, kind="ExternalOutput").ap()
        dbg["zt0"] = nc.dram_tensor("dbg_zt0", [2048, 512], F32, kind="ExternalOutput").ap()
        dbg["qpre"] = nc.dram_tensor("dbg_qpre", [128, 512], F32, kind="ExternalOutput").ap()
        dbg["tcos"] = nc.dram_tensor("dbg_tcos", [128, 512], F32, kind="ExternalOutput").ap()
        dbg["tsin"] = nc.dram_tensor("dbg_tsin", [128, 512], F32, kind="ExternalOutput").ap()
        dbg["oke"] = nc.dram_tensor("dbg_oke", [128, 64], F32, kind="ExternalOutput").ap()
        dbg["mxc"] = nc.dram_tensor("dbg_mxc", [128, 1], F32, kind="ExternalOutput").ap()
        dbg["smc"] = nc.dram_tensor("dbg_smc", [128, 1], F32, kind="ExternalOutput").ap()
        dbg["enz0"] = nc.dram_tensor("dbg_enz0", [128, 64], F32, kind="ExternalOutput").ap()

    rec = Rec(full_edges=paranoid)
    stk = contextlib.ExitStack()
    uid = [0]

    def nm(p):
        uid[0] += 1
        return f"{p}{uid[0]}"

    def sb(name, shape, dt=F32):
        return stk.enter_context(nc.sbuf_tensor(name, list(shape), dt))

    def load(dst_ap, src_ap, wbuf, rbufs=()):
        rec.op("sync", lambda e, d=dst_ap, s=src_ap: e.dma_start(out=d, in_=s),
               reads=rbufs, writes=(wbuf,), queue="dS")

    def store(dst_ap, src_ap, rbuf, wbuf=None):
        rec.op("scalar", lambda e, d=dst_ap, s=src_ap: e.dma_start(out=d, in_=s),
               reads=(rbuf,), writes=(wbuf,) if wbuf is not None else (),
               queue="dA")

    def vop(fn, reads=(), writes=()):
        rec.op("vector", fn, reads=reads, writes=writes)

    def sop(fn, reads=(), writes=()):
        rec.op("scalar", fn, reads=reads, writes=writes)

    def top(fn, reads=(), writes=()):
        rec.op("tensor", fn, reads=reads, writes=writes)

    def split(hi_ap, lo_ap, src_ap, src_bufs, pbuf):
        sop(lambda e, h=hi_ap, s=src_ap: e.activation(h, s, AF.Copy),
            reads=tuple(src_bufs), writes=(pbuf,))
        vop(lambda e, l=lo_ap, s=src_ap, h=hi_ap: e.tensor_sub(l, s, h),
            reads=tuple(src_bufs) + (pbuf,), writes=(pbuf,))

    def mm3(ps_ap, lh, ll, rh, rl, ps_buf, lbuf, rbuf, start, stop):
        top(lambda e: e.matmul(ps_ap, lh, rh, start=start, stop=False),
            reads=(lbuf, rbuf), writes=(ps_buf,))
        top(lambda e: e.matmul(ps_ap, lh, rl, start=False, stop=False),
            reads=(lbuf, rbuf), writes=(ps_buf,))
        top(lambda e: e.matmul(ps_ap, ll, rh, start=False, stop=stop),
            reads=(lbuf, rbuf), writes=(ps_buf,))

    def mm2s(ps_ap, lx, rh, rl, ps_buf, lbuf, rbuf, start, stop):
        top(lambda e: e.matmul(ps_ap, lx, rh, start=start, stop=False),
            reads=(lbuf, rbuf), writes=(ps_buf,))
        top(lambda e: e.matmul(ps_ap, lx, rl, start=False, stop=stop),
            reads=(lbuf, rbuf), writes=(ps_buf,))

    pbk = [stk.enter_context(nc.psum_tensor(f"pb{i}", [128, 512], F32))
           for i in range(8)]
    bp = [Buf() for _ in range(8)]

    ones_row = sb("ones_row", (1, 128), BF16)
    ones_col = sb("ones_col", (128, 1), BF16)
    ident_b = sb("ident_b", (128, 128), BF16)
    hmaskT = sb("hmaskT", (128, 128), BF16)
    masks4 = sb("masks4", (128, 4 * 512), BF16)
    h128su = sb("h128su", (128, 16 * 128), BF16)
    h128b = sb("h128b", (128, 128), BF16)
    gnw_c = sb("gnw_c", (128, 1), F32)
    cs_up = sb("cs_up", (128, 64), F32)
    cs_gt = sb("cs_gt", (128, 64), F32)
    cs_dn = sb("cs_dn", (128, 16), F32)
    su_dp = sb("su_dp", (128, 64), F32)
    c_eps5 = sb("c_eps5", (128, 1), F32)
    c_rms = sb("c_rms", (128, 1), F32)
    tmpg = sb("tmpg", (128, 512), F32)
    bO = Buf()
    bT_ = Buf()
    bA2A1 = Buf(); bAG2 = Buf(); bA2A3 = Buf()
    bIN1 = Buf(); bIN2 = Buf(); bIN3 = Buf()
    bX1A = Buf(); bX1L = Buf(); bX2A = Buf()
    bHID = Buf(); bVDN = Buf(); bV2 = Buf(); bY = Buf()

    def emit_consts():
        vop(lambda e: e.memset(ones_row[:], 1.0), writes=(bO,))
        vop(lambda e: e.memset(ones_col[:], 1.0), writes=(bO,))
        vop(lambda e: e.memset(c_eps5[:], 1e-5), writes=(bO,))
        vop(lambda e: e.memset(c_rms[:], 1.0 / 128.0), writes=(bO,))
        load(tmpg[:, 0:128], ident_in, bT_, rbufs=(bT_,))
        vop(lambda e: e.tensor_copy(ident_b[:], tmpg[:, 0:128]), reads=(bT_,), writes=(bO,))
        load(tmpg[:, 0:128], hmaskT_in, bT_, rbufs=(bT_,))
        vop(lambda e: e.tensor_copy(hmaskT[:], tmpg[:, 0:128]), reads=(bT_,), writes=(bO,))
        load(gnw_c[:], gnw, bO)
        for mi in range(4):
            load(tmpg[:], masks4_in[:, mi * 512:(mi + 1) * 512], bT_, rbufs=(bT_,))
            vop(lambda e, mi=mi: e.tensor_copy(masks4[:, mi * 512:(mi + 1) * 512], tmpg[:]),
                reads=(bT_,), writes=(bO,))
        load(tmpg[:, 0:128], h128_in, bT_, rbufs=(bT_,))
        load(tmpg[:, 128:144], su_ug_fm, bT_, rbufs=(bT_,))
        vop(lambda e: e.tensor_copy(h128b[:], tmpg[:, 0:128]), reads=(bT_,), writes=(bO,))
        for c in range(16):
            vop(lambda e, c=c: e.tensor_scalar(
                h128su[:, c * 128:(c + 1) * 128], tmpg[:, 0:128],
                tmpg[:, 128 + c:129 + c], None, ALU.mult),
                reads=(bT_,), writes=(bO,))
        load(su_dp[:], su_dp_fm, bO)
        load(cs_up[:], wsc_up_fm, bO)
        load(cs_gt[:], wsc_gt_fm, bO)
        load(cs_dn[:], wsc_dn_fm, bO)
        s2048 = 1.0 / float(np.sqrt(2048.0))
        s8192 = 1.0 / float(np.sqrt(8192.0))
        vop(lambda e: e.tensor_scalar_mul(cs_up[:], cs_up[:], s2048), reads=(bO,), writes=(bO,))
        vop(lambda e: e.tensor_scalar_mul(cs_gt[:], cs_gt[:], s2048), reads=(bO,), writes=(bO,))
        vop(lambda e: e.tensor_scalar_mul(cs_dn[:], cs_dn[:], s8192), reads=(bO,), writes=(bO,))

    def emit_vprep(wsrc, vh, vl, signed, vbuf):
        CB = 512
        with contextlib.ExitStack() as ps:
            sbl = lambda n, sh, dt=F32: ps.enter_context(nc.sbuf_tensor(nm(n), list(sh), dt))
            a_t = [sbl("vpa", (128, CB)) for _ in range(NICH)]
            ba = [Buf() for _ in range(NICH)]
            tmp = sbl("vptmp", (128, CB))
            btmp = Buf()
            wld = [sbl("vpw", (128, CB)) for _ in range(3)]
            wph = [sbl("vph", (128, CB), BF16) for _ in range(3)]
            wpl = [sbl("vpl", (128, CB), BF16) for _ in range(3)]
            bw = [Buf() for _ in range(3)]
            bwp = [Buf() for _ in range(3)]
            oh = [sbl("voh", (128, CB), BF16) for _ in range(2)]
            ol = [sbl("vol", (128, CB), BF16) for _ in range(2)]
            boh = [Buf() for _ in range(2)]
            for cb in range(D // CB):
                for ic in range(NICH):
                    k = ic % 3
                    kp = ic % 4
                    load(wld[k][:], wsrc[ic * 128:(ic + 1) * 128, cb * CB:(cb + 1) * CB],
                         bw[k], rbufs=(bw[k], bwp[k]))
                    split(wph[k][:], wpl[k][:], wld[k][:], (bw[k],), bwp[k])
                    mm2s(pbk[kp][:, 0:CB], h128b[:], wph[k][:], wpl[k][:],
                         bp[kp], bO, bwp[k], True, True)
                    sop(lambda e, d=a_t[ic], kp=kp: e.copy(d[:], pbk[kp][:, 0:CB]),
                        reads=(bp[kp],), writes=(ba[ic],))
                h = 1
                while h < NICH:
                    for blk in range(0, NICH, 2 * h):
                        for li in range(h):
                            i0, i1 = blk + li, blk + li + h
                            vop(lambda e, d=tmp, a=a_t[i0], b=a_t[i1]: e.tensor_add(d[:], a[:], b[:]),
                                reads=(ba[i0], ba[i1]), writes=(btmp,))
                            vop(lambda e, a=a_t[i0], b=a_t[i1]: e.tensor_sub(b[:], a[:], b[:]),
                                reads=(ba[i0], ba[i1]), writes=(ba[i1],))
                            a_t[i0], tmp = tmp, a_t[i0]
                            ba[i0], btmp = btmp, ba[i0]
                    h *= 2
                for ic in range(NICH):
                    k = ic % 2
                    if signed:
                        vop(lambda e, s=a_t[ic], ic=ic: e.tensor_scalar(
                            s[:], s[:], su_dp[:, ic:ic + 1], None, ALU.mult),
                            reads=(ba[ic], bO), writes=(ba[ic],))
                    split(oh[k][:], ol[k][:], a_t[ic][:], (ba[ic],), boh[k])
                    oc0 = cb * (CB // 128)
                    store(vh[oc0:oc0 + CB // 128, :, ic, :].transpose([1, 0, 2]),
                          oh[k][:, :], boh[k], wbuf=vbuf)
                    store(vl[oc0:oc0 + CB // 128, :, ic, :].transpose([1, 0, 2]),
                          ol[k][:, :], boh[k], wbuf=vbuf)

    def emit_p0():
        NSEG = T // SEG
        NCHS = SEG // CH
        with contextlib.ExitStack() as ps:
            sbl = lambda n, sh, dt=F32: ps.enter_context(nc.sbuf_tensor(nm(n), list(sh), dt))
            wsb = {}
            bW = Buf()
            ldw = sbl("p0wld", (128, 256))
            bldw = Buf()
            for pname in ("q", "k", "v", "g"):
                for hl in range(2):
                    wsb[(pname, hl)] = (sbl("wsh", (128, NDCH * 128), BF16),
                                        sbl("wsl", (128, NDCH * 128), BF16))
            ws1p = (sbl("ws1h", (128, NDCH * 16), BF16), sbl("ws1l", (128, NDCH * 16), BF16))
            ws2p = (sbl("ws2h", (16, 128), BF16), sbl("ws2l", (16, 128), BF16))
            bs2p = (sbl("bs2h", (1, 128), BF16), sbl("bs2l", (1, 128), BF16))
            for pname, src in (("q", wq0), ("k", wk0), ("v", wv0), ("g", wg0)):
                for ic in range(NDCH):
                    load(ldw[:], src[ic * 128:(ic + 1) * 128, :], bldw, rbufs=(bldw, bW))
                    for hl in range(2):
                        hi_t, lo_t = wsb[(pname, hl)]
                        split(hi_t[:, ic * 128:(ic + 1) * 128],
                              lo_t[:, ic * 128:(ic + 1) * 128],
                              ldw[:, hl * 128:(hl + 1) * 128], (bldw,), bW)
            for ic in range(NDCH):
                load(ldw[:, 0:16], ws1t[ic * 128:(ic + 1) * 128, :], bldw, rbufs=(bldw, bW))
                split(ws1p[0][:, ic * 16:(ic + 1) * 16], ws1p[1][:, ic * 16:(ic + 1) * 16],
                      ldw[:, 0:16], (bldw,), bW)
            load(ldw[0:16, 0:128], ws2t, bldw, rbufs=(bldw, bW))
            split(ws2p[0][:], ws2p[1][:], ldw[0:16, 0:128], (bldw,), bW)
            load(ldw[0:1, 0:128], bs2c, bldw, rbufs=(bldw, bW))
            split(bs2p[0][:], bs2p[1][:], ldw[0:1, 0:128], (bldw,), bW)

            xx = sbl("p0xld", (128, SEG))
            bxl = Buf()
            xp = [(sbl("p0xh", (128, SEG), BF16), sbl("p0xl", (128, SEG), BF16))
                  for _ in range(NDCH)]
            bx = [Buf() for _ in range(NDCH)]
            rope = sbl("p0rope", (128, 4 * SEG))
            brope4 = [Buf() for _ in range(4)]
            qp = [(sbl("qh", (128, SEG), BF16), sbl("ql", (128, SEG), BF16)) for _ in range(2)]
            kp = [(sbl("kh", (128, SEG), BF16), sbl("kl", (128, SEG), BF16)) for _ in range(2)]
            vtm = [(sbl("vth", (128, SEG), BF16), sbl("vtl", (128, SEG), BF16)) for _ in range(2)]
            gg = [sbl("gg", (128, SEG)) for _ in range(2)]
            oo = [sbl("oo", (128, SEG)) for _ in range(2)]
            bq = [Buf(), Buf()]; bk = [Buf(), Buf()]; bv = [Buf(), Buf()]
            bg = [Buf(), Buf()]; bo = [Buf(), Buf()]
            up = (sbl("p0uh", (16, T), BF16), sbl("p0ul", (16, T), BF16))
            bu = Buf()
            vfp = (sbl("p0vfh", (128, SEG), BF16), sbl("p0vfl", (128, SEG), BF16))
            bvf = Buf()
            rop_t = sbl("p0ropt", (128, SEG))
            rot2 = sbl("p0rot2", (128, SEG))
            brt = Buf()
            brot = Buf()
            Sst = [[sbl("Ss", (128, 64)) for _ in range(2)] for _ in range(2)]
            Ust = [[sbl("Us", (64, 128)) for _ in range(2)] for _ in range(2)]
            bSst = [[Buf(), Buf()], [Buf(), Buf()]]
            bUst = [[Buf(), Buf()], [Buf(), Buf()]]
            Sp = [(sbl("Sph", (128, 64), BF16), sbl("Spl", (128, 64), BF16)) for _ in range(2)]
            Up = [(sbl("Uph", (64, 128), BF16), sbl("Upl", (64, 128), BF16)) for _ in range(2)]
            bSp = [Buf(), Buf()]; bUp = [Buf(), Buf()]
            zcar = [(sbl("zch", (1, 64), BF16), sbl("zcl", (1, 64), BF16)) for _ in range(2)]
            bzc = [Buf(), Buf()]
            es_tm = [sbl("es_tm", (128, 64)) for _ in range(2)]
            es_tp = [(sbl("es_tph", (128, 64), BF16), sbl("es_tpl", (128, 64), BF16))
                     for _ in range(2)]
            es_fp = [(sbl("es_fph", (64, 128), BF16), sbl("es_fpl", (64, 128), BF16))
                     for _ in range(2)]
            bes = [Buf(), Buf()]; besf = [Buf(), Buf()]
            at_m = [sbl("at_m", (128, 128)) for _ in range(2)]
            at_p = [(sbl("at_ph", (128, 128), BF16), sbl("at_pl", (128, 128), BF16))
                    for _ in range(2)]
            bat = [Buf(), Buf()]
            w2_p = [(sbl("w2_ph", (128, 128), BF16), sbl("w2_pl", (128, 128), BF16))
                    for _ in range(2)]
            bw2 = [Buf(), Buf()]
            ktm = [(sbl("ktm_h", (128, 128), BF16), sbl("ktm_l", (128, 128), BF16))
                   for _ in range(2)]
            bktm = [Buf(), Buf()]
            zz = [sbl("zz", (128, 64)) for _ in range(2)]
            enz = [sbl("enz", (128, 64)) for _ in range(2)]
            bz = [Buf(), Buf()]; benz = [Buf(), Buf()]
            okv = [sbl("okv", (128, 64)) for _ in range(2)]
            bok = [Buf(), Buf()]
            mxc = [sbl("mxc", (128, 1)) for _ in range(2)]
            smc = [sbl("smc", (128, 1)) for _ in range(2)]
            qden = [sbl("qden", (128, 64)) for _ in range(2)]
            qden2 = [sbl("qden2", (128, 64)) for _ in range(2)]
            bmx = [Buf(), Buf()]
            bqd = [Buf(), Buf()]
            qvp = [sbl("qvp", (128, 64)) for _ in range(2)]
            qv_tp = [(sbl("qv_th", (128, 64), BF16), sbl("qv_tl", (128, 64), BF16))
                     for _ in range(2)]
            qv_fp = [(sbl("qv_fh", (64, 128), BF16), sbl("qv_fl", (64, 128), BF16))
                     for _ in range(2)]
            bqv = [Buf(), Buf()]; bqvf = [Buf(), Buf()]
            o2t = sbl("o2t", (128, SEG))
            bo2 = Buf()
            rrow = sbl("p0rrow", (1, SEG))
            rrow2 = sbl("p0rrow2", (1, SEG))
            brr = Buf()
            sgt = sbl("p0sg", (128, SEG))
            bsg = Buf()
            sp2 = (sbl("sp2h", (128, SEG), BF16), sbl("sp2l", (128, SEG), BF16))
            bsp2 = Buf()
            rrp = (sbl("rrph", (1, SEG), BF16), sbl("rrpl", (1, SEG), BF16))
            brrp = Buf()

            for b in range(B):
                for seg in range(NSEG):
                    t0 = seg * SEG
                    g0 = b * T + t0
                    ib = seg % 2
                    ir = (seg + 1) % 2
                    for ic in range(NDCH):
                        load(xx[:], x0f[ic * 128:(ic + 1) * 128, g0:g0 + SEG], bxl,
                             rbufs=(bxl, bx[ic]))
                        split(xp[ic][0][:], xp[ic][1][:], xx[:], (bxl,), bx[ic])
                    load(rope[:, 0:SEG], cosT[:, t0:t0 + SEG], brope4[0], rbufs=(brope4[0],))
                    load(rope[:, SEG:2 * SEG], sinT[:, t0:t0 + SEG], brope4[1], rbufs=(brope4[1],))
                    load(rope[:, 2 * SEG:3 * SEG], cosqT[:, t0:t0 + SEG], brope4[2], rbufs=(brope4[2],))
                    load(rope[:, 3 * SEG:4 * SEG], sinqT[:, t0:t0 + SEG], brope4[3], rbufs=(brope4[3],))
                    for hl in range(2):
                        for ic in range(NDCH):
                            st, sp_ = ic == 0, ic == NDCH - 1
                            for pj, pbi in (("q", 0), ("k", 1), ("v", 2), ("g", 3)):
                                w = wsb[(pj, hl)]
                                mm3(pbk[pbi][:, 0:SEG],
                                    w[0][:, ic * 128:(ic + 1) * 128],
                                    w[1][:, ic * 128:(ic + 1) * 128],
                                    xp[ic][0][:], xp[ic][1][:],
                                    bp[pbi], bW, bx[ic], st, sp_)
                            if hl == 0:
                                mm3(pbk[4][0:16, 0:SEG],
                                    ws1p[0][:, ic * 16:(ic + 1) * 16],
                                    ws1p[1][:, ic * 16:(ic + 1) * 16],
                                    xp[ic][0][:], xp[ic][1][:],
                                    bp[4], bW, bx[ic], st, sp_)
                        if hl == 0:
                            split(up[0][0:16, t0:t0 + SEG], up[1][0:16, t0:t0 + SEG],
                                  pbk[4][0:16, 0:SEG], (bp[4],), bu)
                        for pbi, dstp, bdst, ci, si in (
                                (0, qp[hl], bq[hl], 2 * SEG, 3 * SEG),
                                (1, kp[hl], bk[hl], 0, SEG)):
                            cosap = rope[:, ci:ci + SEG]
                            sinap = rope[:, si:si + SEG]
                            bc = brope4[2] if pbi == 0 else brope4[0]
                            bs_ = brope4[3] if pbi == 0 else brope4[1]
                            vop(lambda e, i=pbi, c=cosap: e.tensor_tensor(
                                rop_t[:], pbk[i][:, 0:SEG], c, ALU.mult),
                                reads=(bp[pbi], bc), writes=(brt,))
                            vop(lambda e, i=pbi, s=sinap: e.tensor_tensor(
                                pbk[6][:, 0:SEG], pbk[i][:, 0:SEG], s, ALU.mult),
                                reads=(bp[pbi], bs_), writes=(bp[6],))
                            if debug == "full" and b == 0 and seg == 0 and hl == 0 and pbi == 0:
                                sop(lambda e, i=pbi: e.copy(rot2[:], pbk[i][:, 0:SEG]),
                                    reads=(bp[pbi],), writes=(brot,))
                                store(dbg["qpre"][:, 0:SEG], rot2[:], brot)
                                store(dbg["tcos"][:, 0:SEG], rop_t[:], brt)
                                sop(lambda e: e.copy(rot2[:], pbk[6][:, 0:SEG]),
                                    reads=(bp[6],), writes=(brot,))
                                store(dbg["tsin"][:, 0:SEG], rot2[:], brot)
                            vop(lambda e: e.tensor_sub(
                                rot2[0:64, :], rop_t[0:64, :], pbk[6][64:128, 0:SEG]),
                                reads=(brt, bp[6]), writes=(brot,))
                            vop(lambda e: e.tensor_add(
                                rot2[64:128, :], rop_t[64:128, :], pbk[6][0:64, 0:SEG]),
                                reads=(brt, bp[6]), writes=(brot,))
                            split(dstp[0][:], dstp[1][:], rot2[:], (brot,), bdst)
                            if debug == "full" and b == 0 and seg == 0 and hl == 0:
                                key = "q0" if pbi == 0 else "k0"
                                store(dbg[key][:, 0:SEG], rot2[:], brot)
                        split(vfp[0][:], vfp[1][:], pbk[2][:, 0:SEG], (bp[2],), bvf)
                        for q4 in range(SEG // 128):
                            for ip in range(2):
                                top(lambda e, ip=ip, q4=q4: e.transpose(
                                    pbk[5][:, 0:64].bitcast(BF16),
                                    vfp[ip][:, q4 * 128:(q4 + 1) * 128], ident_b[:]),
                                    reads=(bvf, bO), writes=(bp[5],))
                                vop(lambda e, ip=ip, q4=q4, hl=hl: e.tensor_copy(
                                    vtm[hl][ip][:, q4 * 128:(q4 + 1) * 128],
                                    pbk[5][:, 0:64].bitcast(BF16)),
                                    reads=(bp[5],), writes=(bv[hl],))
                        sop(lambda e, hl=hl: e.copy(gg[hl][:], pbk[3][:, 0:SEG]),
                            reads=(bp[3],), writes=(bg[hl],))
                    for hl in range(2):
                        for cc in range(NCHS):
                            tl = cc * CH
                            p = cc % 2
                            pS = 4 if p == 0 else 6
                            pW = 2 if p == 0 else 7
                            first = (seg == 0 and cc == 0)
                            qc = (qp[hl][0][:, tl:tl + CH], qp[hl][1][:, tl:tl + CH])
                            kc = (kp[hl][0][:, tl:tl + CH], kp[hl][1][:, tl:tl + CH])
                            vc = (vtm[hl][0][:, tl:tl + CH], vtm[hl][1][:, tl:tl + CH])
                            uc = (up[0][0:16, t0 + tl:t0 + tl + CH],
                                  up[1][0:16, t0 + tl:t0 + tl + CH])
                            mm3(pbk[pS][:, 0:64], uc[0], uc[1],
                                ws2p[0][:, hl * 64:hl * 64 + 64],
                                ws2p[1][:, hl * 64:hl * 64 + 64],
                                bp[pS], bu, bW, True, False)
                            mm2s(pbk[pS][:, 0:64], ones_row[0:1, 0:CH],
                                 bs2p[0][0:1, hl * 64:hl * 64 + 64],
                                 bs2p[1][0:1, hl * 64:hl * 64 + 64],
                                 bp[pS], bO, bW, False, True)
                            vop(lambda e, p=p, pS=pS: e.tensor_scalar(
                                es_tm[p][:], pbk[pS][:, 0:64], 32.0, -32.0,
                                ALU.min, ALU.max),
                                reads=(bp[pS],), writes=(bes[p],))
                            sop(lambda e, p=p: e.activation(es_tm[p][:], es_tm[p][:], AF.Exp),
                                reads=(bes[p],), writes=(bes[p],))
                            split(es_tp[p][0][:], es_tp[p][1][:], es_tm[p][:],
                                  (bes[p],), bes[p])
                            for ip in range(2):
                                top(lambda e, ip=ip, p=p: e.transpose(
                                    pbk[5][0:64, 0:64].bitcast(BF16), es_tp[p][ip][:],
                                    ident_b[:]),
                                    reads=(bes[p], bO), writes=(bp[5],))
                                vop(lambda e, ip=ip, p=p: e.tensor_copy(
                                    es_fp[p][ip][:], pbk[5][0:64, 0:64].bitcast(BF16)),
                                    reads=(bp[5],), writes=(besf[p],))
                            mm2s(pbk[pS][:, 0:64], hmaskT[:], es_tp[p][0][:], es_tp[p][1][:],
                                 bp[pS], bO, bes[p], True, first)
                            if not first:
                                mm2s(pbk[pS][:, 0:64], ones_row[0:1, 0:CH],
                                     zcar[hl][0][:], zcar[hl][1][:], bp[pS], bO, bzc[hl],
                                     False, True)
                            sop(lambda e, p=p, pS=pS: e.copy(zz[p][:], pbk[pS][:, 0:64]),
                                reads=(bp[pS],), writes=(bz[p],))
                            mm2s(pbk[pS][0:1, 0:64], ones_col[:, 0:1],
                                 es_tp[p][0][:], es_tp[p][1][:], bp[pS], bO, bes[p],
                                 True, first)
                            if not first:
                                mm2s(pbk[pS][0:1, 0:64], ones_row[0:1, 0:1],
                                     zcar[hl][0][:], zcar[hl][1][:], bp[pS], bO,
                                     bzc[hl], False, True)
                            split(zcar[hl][0][:], zcar[hl][1][:], pbk[pS][0:1, 0:64],
                                  (bp[pS],), bzc[hl])
                            vop(lambda e, p=p: e.reciprocal(enz[p][:], zz[p][:]),
                                reads=(bz[p],), writes=(benz[p],))
                            mm3(pbk[pW][:, 0:128], kc[0], kc[1], qc[0], qc[1],
                                bp[pW], bk[hl], bq[hl], True, True)
                            vop(lambda e, p=p, pW=pW: e.tensor_tensor(
                                at_m[p][:], pbk[pW][:, 0:128], hmaskT[:], ALU.mult),
                                reads=(bp[pW], bO), writes=(bat[p],))
                            split(at_p[p][0][:], at_p[p][1][:], at_m[p][:],
                                  (bat[p],), bat[p])
                            mm3(pbk[pS][:, 0:64], at_p[p][0][:], at_p[p][1][:],
                                es_tp[p][0][:], es_tp[p][1][:], bp[pS], bat[p], bes[p],
                                True, first)
                            if not first:
                                mm3(pbk[pS][:, 0:64], qc[0], qc[1],
                                    Sp[hl][0][:], Sp[hl][1][:], bp[pS], bq[hl], bSp[hl],
                                    False, True)
                            vop(lambda e, p=p, pS=pS: e.tensor_tensor(
                                okv[p][:], pbk[pS][:, 0:64], enz[p][:], ALU.mult),
                                reads=(bp[pS], benz[p]), writes=(bok[p],))
                            vop(lambda e, p=p: e.reduce_max(mxc[p][:], okv[p][:],
                                                            axis=AX.X, negate=True),
                                reads=(bok[p],), writes=(bmx[p],))
                            sop(lambda e, p=p: e.activation(okv[p][:], okv[p][:], AF.Exp,
                                                            bias=mxc[p][:]),
                                reads=(bok[p], bmx[p]), writes=(bok[p],))
                            vop(lambda e, p=p: e.reduce_sum(smc[p][:], okv[p][:], axis=AX.X),
                                reads=(bok[p],), writes=(bmx[p],))
                            vop(lambda e, p=p: e.tensor_scalar(
                                qden[p][:], zz[p][:], smc[p][:], None, ALU.mult),
                                reads=(bz[p], bmx[p]), writes=(bqd[p],))
                            vop(lambda e, p=p: e.reciprocal(qden2[p][:], qden[p][:]),
                                reads=(bqd[p],), writes=(bqd[p],))
                            vop(lambda e, p=p: e.tensor_mul(qvp[p][:], okv[p][:],
                                                            qden2[p][:]),
                                reads=(bok[p], bqd[p]), writes=(bqv[p],))
                            split(qv_tp[p][0][:], qv_tp[p][1][:], qvp[p][:],
                                  (bqv[p],), bqv[p])
                            for ip in range(2):
                                top(lambda e, ip=ip, p=p: e.transpose(
                                    pbk[5][0:64, 0:64].bitcast(BF16), qv_tp[p][ip][:],
                                    ident_b[:]),
                                    reads=(bqv[p], bO), writes=(bp[5],))
                                vop(lambda e, ip=ip, p=p: e.tensor_copy(
                                    qv_fp[p][ip][:], pbk[5][0:64, 0:64].bitcast(BF16)),
                                    reads=(bp[5],), writes=(bqvf[p],))
                            mm3(pbk[pW][:, 0:128], es_fp[p][0][:], es_fp[p][1][:],
                                qv_fp[p][0][:], qv_fp[p][1][:], bp[pW], besf[p], bqvf[p],
                                True, True)
                            vop(lambda e, p=p, pW=pW: e.tensor_tensor(
                                at_m[p][:], pbk[pW][:, 0:128], hmaskT[:], ALU.mult),
                                reads=(bp[pW], bO), writes=(bw2[p],))
                            split(w2_p[p][0][:], w2_p[p][1][:], at_m[p][:],
                                  (bw2[p],), bw2[p])
                            mm3(pbk[3][:, 0:128], vc[0], vc[1], w2_p[p][0][:], w2_p[p][1][:],
                                bp[3], bv[hl], bw2[p], True, first)
                            if not first:
                                mm3(pbk[3][:, 0:128], Up[hl][0][:], Up[hl][1][:],
                                    qv_fp[p][0][:], qv_fp[p][1][:], bp[3], bUp[hl],
                                    bqvf[p], False, True)
                            sop(lambda e, tl=tl, hl=hl: e.copy(oo[hl][:, tl:tl + CH],
                                                               pbk[3][:, 0:128]),
                                reads=(bp[3],), writes=(bo[hl],))
                            for ip in range(2):
                                top(lambda e, ip=ip, k_=kc: e.transpose(
                                    pbk[5][:, 0:64].bitcast(BF16), k_[ip], ident_b[:]),
                                    reads=(bk[hl], bO), writes=(bp[5],))
                                vop(lambda e, ip=ip, p=p: e.tensor_copy(
                                    ktm[p][ip][:], pbk[5][:, 0:64].bitcast(BF16)),
                                    reads=(bp[5],), writes=(bktm[p],))
                            mm3(pbk[0][:, 0:64], ktm[p][0][:], ktm[p][1][:],
                                es_tp[p][0][:], es_tp[p][1][:], bp[0], bktm[p], bes[p],
                                True, True)
                            mm3(pbk[1][0:64, 0:128], es_tp[p][0][:], es_tp[p][1][:],
                                vc[0], vc[1], bp[1], bes[p], bv[hl], True, True)
                            cidx = seg * NCHS + cc
                            i_b, i_r = cidx % 2, (cidx + 1) % 2
                            if first:
                                sop(lambda e, d=Sst[hl][i_r]: e.copy(d[:], pbk[0][:, 0:64]),
                                    reads=(bp[0],), writes=(bSst[hl][i_r],))
                                sop(lambda e, d=Ust[hl][i_r]: e.copy(d[:], pbk[1][0:64, 0:128]),
                                    reads=(bp[1],), writes=(bUst[hl][i_r],))
                            else:
                                vop(lambda e, d=Sst[hl][i_r], s=Sst[hl][i_b]: e.tensor_tensor(
                                    d[:], s[:], pbk[0][:, 0:64], ALU.add),
                                    reads=(bp[0], bSst[hl][i_b]), writes=(bSst[hl][i_r],))
                                vop(lambda e, d=Ust[hl][i_r], s=Ust[hl][i_b]: e.tensor_tensor(
                                    d[:], s[:], pbk[1][0:64, 0:128], ALU.add),
                                    reads=(bp[1], bUst[hl][i_b]), writes=(bUst[hl][i_r],))
                            split(Sp[hl][0][:], Sp[hl][1][:], Sst[hl][i_r][:],
                                  (bSst[hl][i_r],), bSp[hl])
                            split(Up[hl][0][:], Up[hl][1][:], Ust[hl][i_r][:],
                                  (bUst[hl][i_r],), bUp[hl])
                        sop(lambda e, hl=hl: e.activation(o2t[:], oo[hl][:], AF.Square),
                            reads=(bo[hl],), writes=(bo2,))
                        split(sp2[0][:], sp2[1][:], o2t[:], (bo2,), bsp2)
                        mm2s(pbk[4][0:1, 0:SEG], ones_col[:, 0:1], sp2[0][:], sp2[1][:],
                             bp[4], bO, bsp2, True, True)
                        sop(lambda e: e.activation(rrow[:], pbk[4][0:1, 0:SEG], AF.Sqrt,
                                                   bias=c_eps5[0:1, :], scale=c_rms[0:1, :]),
                            reads=(bp[4], bO), writes=(brr,))
                        vop(lambda e: e.reciprocal(rrow2[:], rrow[:]),
                            reads=(brr,), writes=(brr,))
                        split(rrp[0][:], rrp[1][:], rrow2[:], (brr,), brrp)
                        mm2s(pbk[4][:, 0:SEG], ones_row[0:1, :], rrp[0][:], rrp[1][:],
                             bp[4], bO, brrp, True, True)
                        vop(lambda e, hl=hl: e.scalar_tensor_tensor(
                            o2t[:], oo[hl][:], gnw_c[:, 0:1], pbk[4][:, 0:SEG],
                            ALU.mult, ALU.mult),
                            reads=(bo[hl], bp[4], bO), writes=(bo2,))
                        sop(lambda e, hl=hl: e.activation(sgt[:], gg[hl][:], AF.Silu),
                            reads=(bg[hl],), writes=(bsg,))
                        vop(lambda e: e.tensor_mul(o2t[:], o2t[:], sgt[:]),
                            reads=(bo2, bsg), writes=(bo2,))
                        if debug == "full" and b == 0 and seg == 0 and hl == 0:
                            store(dbg["oo0"][:, 0:SEG], oo[hl][:], bo[hl])
                        blk = g0 // L
                        coff = g0 % L
                        store(a2a1_in[blk * 256 + hl * 128:blk * 256 + (hl + 1) * 128,
                                      coff:coff + SEG], o2t[:], bo2, wbuf=bIN1)
                        if debug:
                            store(dbg["og"][hl * 128:(hl + 1) * 128, g0:g0 + SEG],
                                  o2t[:], bo2)

    def emit_collective(kind, in_ap, out_ap, rbuf, wbuf):
        def fn(e):
            with allow_ici_all_to_all():
                return e.collective_compute(
                    kind, ALU.bypass, replica_groups=[list(range(NCORES))],
                    ins=[in_ap.opt()], outs=[out_ap.opt()])
        rec.op("gpsimd", fn, reads=(rbuf,), writes=(wbuf,), queue="cc")

    def emit_proj_ln(og_src, bog, wo_tl_ap, xres_ap, bxres, xa_dst, bxa, dbg_key):
        with contextlib.ExitStack() as ps:
            sbl = lambda n, sh, dt=F32: ps.enter_context(nc.sbuf_tensor(nm(n), list(sh), dt))
            og_p = [(sbl("ogh", (128, 512), BF16), sbl("ogl", (128, 512), BF16))
                    for _ in range(NDCH)]
            bogp = [Buf() for _ in range(NDCH)]
            ogld = sbl("ogld", (128, 512))
            bold = Buf()
            wsl = [sbl("wosl", (128, D)) for _ in range(2)]
            wsp = [(sbl("woh", (128, D), BF16), sbl("wol", (128, D), BF16))
                   for _ in range(2)]
            bws = [Buf(), Buf()]
            bwsp = [Buf(), Buf()]
            xrt = [sbl("xrt", (128, 512)) for _ in range(2)]
            bxrt = [Buf(), Buf()]
            r_t = [sbl("rt_", (128, 512)) for _ in range(NDCH)]
            br = [Buf() for _ in range(NDCH)]
            rp = (sbl("rph", (128, 512), BF16), sbl("rpl", (128, 512), BF16))
            brp = Buf()
            sq_t = sbl("sqt", (128, 512))
            bsq = Buf()
            stat = sbl("stat", (1, 4 * 512))
            bst = Buf()
            stp = (sbl("stph", (1, 1024), BF16), sbl("stpl", (1, 1024), BF16))
            bstp = Buf()
            for tk in range(NT5):
                c0 = tk * 512
                for ic in range(NDCH):
                    load(ogld[:], og_src[ic * 128:(ic + 1) * 128, c0:c0 + 512], bold,
                         rbufs=(bold, bogp[ic], bog))
                    split(og_p[ic][0][:], og_p[ic][1][:], ogld[:], (bold,), bogp[ic])
                for oc in range(NDCH):
                    k = oc % 2
                    load(wsl[k][:], wo_tl_ap[oc], bws[k], rbufs=(bws[k], bwsp[k]))
                    split(wsp[k][0][:], wsp[k][1][:], wsl[k][:], (bws[k],), bwsp[k])
                    for ic in range(NDCH):
                        mm3(pbk[k][:, 0:512],
                            wsp[k][0][:, ic * 128:(ic + 1) * 128],
                            wsp[k][1][:, ic * 128:(ic + 1) * 128],
                            og_p[ic][0][:], og_p[ic][1][:],
                            bp[k], bwsp[k], bogp[ic], ic == 0, ic == NDCH - 1)
                    load(xrt[k][:], xres_ap[oc * 128:(oc + 1) * 128, c0:c0 + 512], bxrt[k],
                         rbufs=(bxrt[k],) + ((bxres,) if bxres is not None else ()))
                    vop(lambda e, oc=oc, k=k: e.tensor_tensor(r_t[oc][:], pbk[k][:, 0:512],
                                                              xrt[k][:], ALU.add),
                        reads=(bp[k], bxrt[k]), writes=(br[oc],))
                    split(rp[0][:], rp[1][:], r_t[oc][:], (br[oc],), brp)
                    mm2s(pbk[2][0:1, 0:512], ones_col[:, 0:1], rp[0][:], rp[1][:],
                         bp[2], bO, brp, oc == 0, oc == NDCH - 1)
                    sop(lambda e, oc=oc: e.activation(sq_t[:], r_t[oc][:], AF.Square),
                        reads=(br[oc],), writes=(bsq,))
                    split(rp[0][:], rp[1][:], sq_t[:], (bsq,), brp)
                    mm2s(pbk[3][0:1, 0:512], ones_col[:, 0:1], rp[0][:], rp[1][:],
                         bp[3], bO, brp, oc == 0, oc == NDCH - 1)
                vop(lambda e: e.tensor_scalar_mul(stat[0:1, 0:512], pbk[2][0:1, 0:512],
                                                  1.0 / float(D)),
                    reads=(bp[2],), writes=(bst,))
                vop(lambda e: e.tensor_scalar_mul(stat[0:1, 512:1024], pbk[3][0:1, 0:512],
                                                  1.0 / float(D)),
                    reads=(bp[3],), writes=(bst,))
                vop(lambda e: e.tensor_mul(stat[0:1, 1024:1536], stat[0:1, 0:512],
                                           stat[0:1, 0:512]),
                    reads=(bst,), writes=(bst,))
                vop(lambda e: e.tensor_sub(stat[0:1, 1024:1536], stat[0:1, 512:1024],
                                           stat[0:1, 1024:1536]),
                    reads=(bst,), writes=(bst,))
                sop(lambda e: e.activation(stat[0:1, 1024:1536], stat[0:1, 1024:1536],
                                           AF.Sqrt, bias=c_eps5[0:1, :]),
                    reads=(bst, bO), writes=(bst,))
                vop(lambda e: e.reciprocal(stat[0:1, 512:1024], stat[0:1, 1024:1536]),
                    reads=(bst,), writes=(bst,))
                vop(lambda e: e.tensor_mul(stat[0:1, 1536:2048], stat[0:1, 0:512],
                                           stat[0:1, 512:1024]),
                    reads=(bst,), writes=(bst,))
                split(stp[0][0:1, 0:512], stp[1][0:1, 0:512], stat[0:1, 512:1024],
                      (bst,), bstp)
                split(stp[0][0:1, 512:1024], stp[1][0:1, 512:1024],
                      stat[0:1, 1536:2048], (bst,), bstp)
                mm2s(pbk[4][:, 0:512], ones_row[0:1, :], stp[0][0:1, 0:512],
                     stp[1][0:1, 0:512], bp[4], bO, bstp, True, True)
                mm2s(pbk[5][:, 0:512], ones_row[0:1, :], stp[0][0:1, 512:1024],
                     stp[1][0:1, 512:1024], bp[5], bO, bstp, True, True)
                for oc in range(NDCH):
                    vop(lambda e, oc=oc: e.tensor_tensor(r_t[oc][:], r_t[oc][:],
                                                         pbk[4][:, 0:512], ALU.mult),
                        reads=(br[oc], bp[4]), writes=(br[oc],))
                    vop(lambda e, oc=oc: e.tensor_sub(r_t[oc][:], r_t[oc][:],
                                                      pbk[5][:, 0:512]),
                        reads=(br[oc], bp[5]), writes=(br[oc],))
                    store(xa_dst[oc * 128:(oc + 1) * 128, c0:c0 + 512], r_t[oc][:],
                          br[oc], wbuf=bxa)
                    if debug and dbg_key is not None:
                        store(dbg[dbg_key][oc * 128:(oc + 1) * 128, c0:c0 + 512],
                              r_t[oc][:], br[oc])

    def emit_ffn(xa_src, bxa, fwht_in, upw, gtw, csu, csg, vhh, vll, bvv, dscale_col,
                 out_stores, dbg_key):
        with contextlib.ExitStack() as outer:
            sblo = lambda n, sh, dt=F32: outer.enter_context(nc.sbuf_tensor(nm(n), list(sh), dt))
            zh = sblo("zh", (128, NDCH * L), BF16)
            zl = sblo("zl", (128, NDCH * L), BF16)
            bz_ = Buf()
            ps = contextlib.ExitStack()
            sbl = lambda n, sh, dt=F32: ps.enter_context(nc.sbuf_tensor(nm(n), list(sh), dt))
            zt = [sbl("zt", (128, 512)) for _ in range(NDCH)]
            bzt = [Buf() for _ in range(NDCH)]
            ztmp = sbl("ztmp", (128, 512))
            bztmp = Buf()
            xin = sbl("xin", (128, 512))
            bxin = Buf()
            xinp = (sbl("xinh", (128, 512), BF16), sbl("xinl", (128, 512), BF16))
            bxinp = Buf()
            for tk in range(NT5):
                c0 = tk * 512
                if fwht_in:
                    for ic in range(NDCH):
                        load(xin[:], xa_src[ic * 128:(ic + 1) * 128, c0:c0 + 512], bxin,
                             rbufs=(bxin, bxinp, bxa))
                        split(xinp[0][:], xinp[1][:], xin[:], (bxin,), bxinp)
                        mm2s(pbk[0][:, 0:512], h128su[:, ic * 128:(ic + 1) * 128],
                             xinp[0][:], xinp[1][:], bp[0], bO, bxinp, True, True)
                        sop(lambda e, ic=ic: e.copy(zt[ic][:], pbk[0][:, 0:512]),
                            reads=(bp[0],), writes=(bzt[ic],))
                    h = 1
                    while h < NDCH:
                        for blk in range(0, NDCH, 2 * h):
                            for li in range(h):
                                i0, i1 = blk + li, blk + li + h
                                vop(lambda e, a=zt[i0], b=zt[i1]: e.tensor_add(
                                    ztmp[:], a[:], b[:]),
                                    reads=(bzt[i0], bzt[i1]), writes=(bztmp,))
                                vop(lambda e, a=zt[i0], b=zt[i1]: e.tensor_sub(
                                    b[:], a[:], b[:]),
                                    reads=(bzt[i0], bzt[i1]), writes=(bzt[i1],))
                                vop(lambda e, a=zt[i0]: e.tensor_copy(a[:], ztmp[:]),
                                    reads=(bztmp,), writes=(bzt[i0],))
                        h *= 2
                    for ic in range(NDCH):
                        if debug and fwht_in and tk == 0:
                            store(dbg["zt0"][ic * 128:(ic + 1) * 128, :], zt[ic][:],
                                  bzt[ic])
                        split(zh[:, (ic * NT5 + tk) * 512:(ic * NT5 + tk) * 512 + 512],
                              zl[:, (ic * NT5 + tk) * 512:(ic * NT5 + tk) * 512 + 512],
                              zt[ic][:], (bzt[ic],), bz_)
                else:
                    for ic in range(NDCH):
                        load(xin[:], xa_src[ic * 128:(ic + 1) * 128, c0:c0 + 512], bxin,
                             rbufs=(bxin, bxa))
                        split(zh[:, (ic * NT5 + tk) * 512:(ic * NT5 + tk) * 512 + 512],
                              zl[:, (ic * NT5 + tk) * 512:(ic * NT5 + tk) * 512 + 512],
                              xin[:], (bxin,), bz_)
            ps.close()
            rec.barrier()
            ps = contextlib.ExitStack()
            sbl = lambda n, sh, dt=F32: ps.enter_context(nc.sbuf_tensor(nm(n), list(sh), dt))
            wld = [sbl("fwld", (128, D // 2)) for _ in range(4)]
            bwl = [Buf() for _ in range(4)]
            wp = [[(sbl("fwh", (128, D), BF16), sbl("fwl", (128, D), BF16)) for _ in range(2)]
                  for _ in range(2)]
            bwp = [[Buf(), Buf()], [Buf(), Buf()]]
            htile = [sbl("htile", (128, 512)) for _ in range(2)]
            bht = [Buf(), Buf()]
            hp = [(sbl("hph", (128, 512), BF16), sbl("hpl", (128, 512), BF16))
                  for _ in range(2)]
            bhp = [Buf(), Buf()]
            sil = [sbl("sil", (128, 512)) for _ in range(2)]
            bsil = [Buf(), Buf()]
            for oc in range(NICH):
                k = oc % 2
                for hf in range(2):
                    hsl = slice(hf * (D // 2), (hf + 1) * (D // 2))
                    load(wld[k][:], upw[oc, :, hsl], bwl[k], rbufs=(bwl[k], bwp[0][k]))
                    split(wp[0][k][0][:, hsl], wp[0][k][1][:, hsl], wld[k][:],
                          (bwl[k],), bwp[0][k])
                    load(wld[2 + k][:], gtw[oc, :, hsl], bwl[2 + k],
                         rbufs=(bwl[2 + k], bwp[1][k]))
                    split(wp[1][k][0][:, hsl], wp[1][k][1][:, hsl], wld[2 + k][:],
                          (bwl[2 + k],), bwp[1][k])
                for ic in range(NDCH):
                    st, sp_ = ic == 0, ic == NDCH - 1
                    for w_i, base in ((0, 0), (1, 4)):
                        wh = wp[w_i][k][0][:, ic * 128:(ic + 1) * 128]
                        wl = wp[w_i][k][1][:, ic * 128:(ic + 1) * 128]
                        bwpk = bwp[w_i][k]
                        for tk in range(NT5):
                            zc0 = zh[:, (ic * NT5 + tk) * 512:(ic * NT5 + tk) * 512 + 512]
                            zc1 = zl[:, (ic * NT5 + tk) * 512:(ic * NT5 + tk) * 512 + 512]
                            top(lambda e, wh=wh, z=zc0, p=base + tk, st=st: e.matmul(
                                pbk[p][:, 0:512], wh, z, start=st, stop=False),
                                reads=(bwpk, bz_), writes=(bp[base + tk],))
                            top(lambda e, wh=wh, z=zc1, p=base + tk: e.matmul(
                                pbk[p][:, 0:512], wh, z, start=False, stop=False),
                                reads=(bwpk, bz_), writes=(bp[base + tk],))
                        for tk in range(NT5):
                            zc0 = zh[:, (ic * NT5 + tk) * 512:(ic * NT5 + tk) * 512 + 512]
                            top(lambda e, wl=wl, z=zc0, p=base + tk, sp_=sp_: e.matmul(
                                pbk[p][:, 0:512], wl, z, start=False, stop=sp_),
                                reads=(bwpk, bz_), writes=(bp[base + tk],))
                for tk in range(NT5):
                    c0 = tk * 512
                    r = tk % 2
                    pu, pg = tk, 4 + tk
                    if csu is not None:
                        vop(lambda e, oc=oc, r=r, pg=pg: e.tensor_scalar(
                            sil[r][:], pbk[pg][:, 0:512], csg[:, oc:oc + 1], None, ALU.mult),
                            reads=(bp[pg], bO), writes=(bsil[r],))
                        sop(lambda e, r=r: e.activation(sil[r][:], sil[r][:], AF.Silu),
                            reads=(bsil[r],), writes=(bsil[r],))
                        vop(lambda e, oc=oc, r=r, pu=pu: e.scalar_tensor_tensor(
                            htile[r][:], pbk[pu][:, 0:512], csu[:, oc:oc + 1], sil[r][:],
                            ALU.mult, ALU.mult),
                            reads=(bp[pu], bsil[r], bO), writes=(bht[r],))
                    else:
                        sop(lambda e, r=r, pg=pg: e.activation(sil[r][:], pbk[pg][:, 0:512],
                                                               AF.Silu),
                            reads=(bp[pg],), writes=(bsil[r],))
                        vop(lambda e, r=r, pu=pu: e.tensor_tensor(
                            htile[r][:], pbk[pu][:, 0:512], sil[r][:], ALU.mult),
                            reads=(bp[pu], bsil[r]), writes=(bht[r],))
                    split(hp[r][0][:], hp[r][1][:], htile[r][:], (bht[r],), bhp[r])
                    store(hid_h[oc, :, c0:c0 + 512], hp[r][0][:], bhp[r], wbuf=bHID)
                    store(hid_l[oc, :, c0:c0 + 512], hp[r][1][:], bhp[r], wbuf=bHID)
            ps.close()
        rec.barrier()
        TKB = 256
        NTB = L // TKB
        with contextlib.ExitStack() as ps:
            sbl = lambda n, sh, dt=F32: ps.enter_context(nc.sbuf_tensor(nm(n), list(sh), dt))
            hpair = (sbl("hdh", (128, NICH * TKB), BF16), sbl("hdl", (128, NICH * TKB), BF16))
            bhd = [Buf() for _ in range(16)]
            vsl = [(sbl("vslh", (128, NICH * 128), BF16), sbl("vsll", (128, NICH * 128), BF16))
                   for _ in range(2)]
            bvsl = [[Buf(), Buf()], [Buf(), Buf()]]
            xrt = [sbl("dxrt", (128, TKB)) for _ in range(2)]
            bxrt = [Buf(), Buf()]
            r_t = [sbl("drt", (128, TKB)) for _ in range(NDCH)]
            br = [Buf() for _ in range(NDCH)]
            rp = (sbl("drph", (128, TKB), BF16), sbl("drpl", (128, TKB), BF16))
            brp = Buf()
            sq_t = sbl("dsqt", (128, TKB))
            bsq = Buf()
            stat = sbl("dstat", (1, 4 * TKB))
            bst = Buf()
            stp = (sbl("dstph", (1, 2 * TKB), BF16), sbl("dstpl", (1, 2 * TKB), BF16))
            bstp = Buf()
            xa_res_src, xa_res_buf = xa_src, bxa
            for tb in range(NTB):
                c0 = tb * TKB
                for grp in range(8):
                    load(hpair[0][:, grp * 8 * TKB:(grp + 1) * 8 * TKB],
                         hid_h[grp * 8:(grp + 1) * 8, :, c0:c0 + TKB].transpose([1, 0, 2]),
                         bhd[grp], rbufs=(bhd[grp], bHID))
                    load(hpair[1][:, grp * 8 * TKB:(grp + 1) * 8 * TKB],
                         hid_l[grp * 8:(grp + 1) * 8, :, c0:c0 + TKB].transpose([1, 0, 2]),
                         bhd[8 + grp], rbufs=(bhd[8 + grp], bHID)) 
                for oc in range(NDCH):
                    k = oc % 2
                    load(vsl[k][0][:], vhh[oc, :, :, :].rearrange("p a b -> p (a b)"),
                         bvsl[k][0], rbufs=(bvsl[k][0], bvv))
                    load(vsl[k][1][:], vll[oc, :, :, :].rearrange("p a b -> p (a b)"),
                         bvsl[k][1], rbufs=(bvsl[k][1], bvv))
                    for ic in range(NICH):
                        hc = (hpair[0][:, ic * TKB:(ic + 1) * TKB],
                              hpair[1][:, ic * TKB:(ic + 1) * TKB])
                        bh1 = bhd[ic // 8]
                        bh2 = bhd[8 + ic // 8]
                        top(lambda e, k=k, ic=ic, hc=hc, oc=oc: e.matmul(
                            pbk[k][:, 0:TKB], vsl[k][0][:, ic * 128:(ic + 1) * 128],
                            hc[0], start=ic == 0, stop=False),
                            reads=(bvsl[k][0], bh1), writes=(bp[k],))
                        top(lambda e, k=k, ic=ic, hc=hc: e.matmul(
                            pbk[k][:, 0:TKB], vsl[k][0][:, ic * 128:(ic + 1) * 128],
                            hc[1], start=False, stop=False),
                            reads=(bvsl[k][0], bh2), writes=(bp[k],))
                        top(lambda e, k=k, ic=ic, hc=hc: e.matmul(
                            pbk[k][:, 0:TKB], vsl[k][1][:, ic * 128:(ic + 1) * 128],
                            hc[0], start=False, stop=ic == NICH - 1),
                            reads=(bvsl[k][1], bh1), writes=(bp[k],))
                    load(xrt[k][:], xa_res_src[oc * 128:(oc + 1) * 128, c0:c0 + TKB],
                         bxrt[k], rbufs=(bxrt[k], xa_res_buf))
                    if dscale_col is not None:
                        vop(lambda e, oc=oc, k=k: e.scalar_tensor_tensor(
                            r_t[oc][:], pbk[k][:, 0:TKB], dscale_col[:, oc:oc + 1],
                            xrt[k][:], ALU.mult, ALU.add),
                            reads=(bp[k], bxrt[k], bO), writes=(br[oc],))
                    else:
                        vop(lambda e, oc=oc, k=k: e.scalar_tensor_tensor(
                            r_t[oc][:], pbk[k][:, 0:TKB],
                            1.0 / float(np.sqrt(8192.0)), xrt[k][:], ALU.mult, ALU.add),
                            reads=(bp[k], bxrt[k]), writes=(br[oc],))
                    split(rp[0][:], rp[1][:], r_t[oc][:], (br[oc],), brp)
                    mm2s(pbk[2][0:1, 0:TKB], ones_col[:, 0:1], rp[0][:], rp[1][:],
                         bp[2], bO, brp, oc == 0, oc == NDCH - 1)
                    sop(lambda e, oc=oc: e.activation(sq_t[:], r_t[oc][:], AF.Square),
                        reads=(br[oc],), writes=(bsq,))
                    split(rp[0][:], rp[1][:], sq_t[:], (bsq,), brp)
                    mm2s(pbk[3][0:1, 0:TKB], ones_col[:, 0:1], rp[0][:], rp[1][:],
                         bp[3], bO, brp, oc == 0, oc == NDCH - 1)
                vop(lambda e: e.tensor_scalar_mul(stat[0:1, 0:TKB], pbk[2][0:1, 0:TKB],
                                                  1.0 / float(D)),
                    reads=(bp[2],), writes=(bst,))
                vop(lambda e: e.tensor_scalar_mul(stat[0:1, TKB:2 * TKB],
                                                  pbk[3][0:1, 0:TKB], 1.0 / float(D)),
                    reads=(bp[3],), writes=(bst,))
                vop(lambda e: e.tensor_mul(stat[0:1, 2 * TKB:3 * TKB], stat[0:1, 0:TKB],
                                           stat[0:1, 0:TKB]),
                    reads=(bst,), writes=(bst,))
                vop(lambda e: e.tensor_sub(stat[0:1, 2 * TKB:3 * TKB],
                                           stat[0:1, TKB:2 * TKB],
                                           stat[0:1, 2 * TKB:3 * TKB]),
                    reads=(bst,), writes=(bst,))
                sop(lambda e: e.activation(stat[0:1, 2 * TKB:3 * TKB],
                                           stat[0:1, 2 * TKB:3 * TKB], AF.Sqrt,
                                           bias=c_eps5[0:1, :]),
                    reads=(bst, bO), writes=(bst,))
                vop(lambda e: e.reciprocal(stat[0:1, TKB:2 * TKB],
                                           stat[0:1, 2 * TKB:3 * TKB]),
                    reads=(bst,), writes=(bst,))
                vop(lambda e: e.tensor_mul(stat[0:1, 3 * TKB:4 * TKB], stat[0:1, 0:TKB],
                                           stat[0:1, TKB:2 * TKB]),
                    reads=(bst,), writes=(bst,))
                split(stp[0][0:1, 0:TKB], stp[1][0:1, 0:TKB], stat[0:1, TKB:2 * TKB],
                      (bst,), bstp)
                split(stp[0][0:1, TKB:2 * TKB], stp[1][0:1, TKB:2 * TKB],
                      stat[0:1, 3 * TKB:4 * TKB], (bst,), bstp)
                mm2s(pbk[4][:, 0:TKB], ones_row[0:1, :], stp[0][0:1, 0:TKB],
                     stp[1][0:1, 0:TKB], bp[4], bO, bstp, True, True)
                mm2s(pbk[5][:, 0:TKB], ones_row[0:1, :], stp[0][0:1, TKB:2 * TKB],
                     stp[1][0:1, TKB:2 * TKB], bp[5], bO, bstp, True, True)
                for oc in range(NDCH):
                    vop(lambda e, oc=oc: e.tensor_tensor(r_t[oc][:], r_t[oc][:],
                                                         pbk[4][:, 0:TKB], ALU.mult),
                        reads=(br[oc], bp[4]), writes=(br[oc],))
                    vop(lambda e, oc=oc: e.tensor_sub(r_t[oc][:], r_t[oc][:],
                                                      pbk[5][:, 0:TKB]),
                        reads=(br[oc], bp[5]), writes=(br[oc],))
                    for dst_ap_fn, wb in out_stores:
                        store(dst_ap_fn(oc, c0, TKB), r_t[oc][:], br[oc], wbuf=wb)
                    if debug and dbg_key is not None:
                        store(dbg[dbg_key][oc * 128:(oc + 1) * 128, c0:c0 + TKB],
                              r_t[oc][:], br[oc])

    def emit_p2():
        NTS = T // 512
        with contextlib.ExitStack() as ps:
            sbl = lambda n, sh, dt=F32: ps.enter_context(nc.sbuf_tensor(nm(n), list(sh), dt))
            wsb = {}
            bW = Buf()
            ldw = sbl("p2wld", (128, 256))
            bldw = Buf()
            for pname in ("q", "k", "v"):
                wsb[pname] = (sbl("w2h_", (128, NDCH * 128), BF16),
                              sbl("w2l_", (128, NDCH * 128), BF16))

            def p2_load_weights(hl):
                for pname, src in (("q", wq1), ("k", wk1), ("v", wv1)):
                    for ic in range(NDCH):
                        load(ldw[:, 0:128], src[ic * 128:(ic + 1) * 128,
                                                hl * 128:(hl + 1) * 128], bldw,
                             rbufs=(bldw, bW))
                        hi_t, lo_t = wsb[pname]
                        split(hi_t[:, ic * 128:(ic + 1) * 128],
                              lo_t[:, ic * 128:(ic + 1) * 128],
                              ldw[:, 0:128], (bldw,), bW)
            xx = sbl("p2xld", (128, 512))
            bxl = Buf()
            xpp = (sbl("p2xh", (128, 512), BF16), sbl("p2xl", (128, 512), BF16))
            bxp = Buf()
            rope = sbl("p2rope", (128, 2 * 512))
            brope = Buf()
            rop_t = sbl("p2ropt", (128, 512))
            brt = Buf()
            q32 = sbl("p2q", (128, T))
            bq32 = Buf()
            kpr = (sbl("p2kh", (128, T), BF16), sbl("p2kl", (128, T), BF16))
            bkp = Buf()
            vtm = (sbl("p2vth", (128, T), BF16), sbl("p2vtl", (128, T), BF16))
            bvt = Buf()
            vfp = (sbl("p2vfh", (128, 512), BF16), sbl("p2vfl", (128, 512), BF16))
            bvf = Buf()
            qbp = [(sbl("p2qbh", (128, 512), BF16), sbl("p2qbl", (128, 512), BF16))
                   for _ in range(2)]
            bqb = [Buf(), Buf()]
            w1t = sbl("p2w1", (128, 512))
            bw1 = Buf()
            w2t = [sbl("p2w2", (128, 512)) for _ in range(4)]
            bw2_ = [Buf() for _ in range(4)]
            w2pp = [(sbl("p2w2h", (128, 512), BF16), sbl("p2w2l", (128, 512), BF16))
                    for _ in range(4)]
            bw2p = [Buf() for _ in range(4)]
            wq1t = [sbl("p2w1r", (128, 512)) for _ in range(4)]
            bwq1 = [Buf() for _ in range(4)]
            den_s = [sbl("p2dens", (1, 512)) for _ in range(2)]
            bden_s = [Buf(), Buf()]
            den = sbl("p2den", (1, 512))
            den2 = sbl("p2den2", (1, 512))
            bden = Buf()
            denp = (sbl("p2dh", (1, 512), BF16), sbl("p2dl", (1, 512), BF16))
            bdenp = Buf()
            dbc = sbl("p2dbc", (128, 512))
            bdbc = Buf()
            ot = sbl("p2ot", (128, 512))
            bot = Buf()
            for b in range(B):
                for hl in range(2):
                    p2_load_weights(hl)
                    for ts in range(NTS):
                        t0 = ts * 512
                        g0 = b * T + t0
                        blk = g0 // L
                        co = g0 % L
                        load(rope[:, 0:512], cosT[:, t0:t0 + 512], brope, rbufs=(brope,))
                        load(rope[:, 512:1024], sinT[:, t0:t0 + 512], brope, rbufs=(brope,))
                        for ic in range(NDCH):
                            load(xx[:], ag2_out[blk * D + ic * 128:blk * D + (ic + 1) * 128,
                                                co:co + 512], bxl,
                                 rbufs=(bxl, bxp, bAG2))
                            split(xpp[0][:], xpp[1][:], xx[:], (bxl,), bxp)
                            for pj, pbi in (("q", 0), ("k", 1), ("v", 2)):
                                w = wsb[pj]
                                mm3(pbk[pbi][:, 0:512],
                                    w[0][:, ic * 128:(ic + 1) * 128],
                                    w[1][:, ic * 128:(ic + 1) * 128],
                                    xpp[0][:], xpp[1][:],
                                    bp[pbi], bW, bxp, ic == 0, ic == NDCH - 1)
                        for pbi in (0, 1):
                            vop(lambda e, i=pbi: e.tensor_tensor(
                                rop_t[:], pbk[i][:, 0:512], rope[:, 0:512], ALU.mult),
                                reads=(bp[pbi], brope), writes=(brt,))
                            vop(lambda e, i=pbi: e.tensor_tensor(
                                pbk[6][:, 0:512], pbk[i][:, 0:512], rope[:, 512:1024],
                                ALU.mult),
                                reads=(bp[pbi], brope), writes=(bp[6],))
                            if pbi == 0:
                                vop(lambda e, t0=t0: e.tensor_sub(
                                    q32[0:64, t0:t0 + 512], rop_t[0:64, :],
                                    pbk[6][64:128, 0:512]),
                                    reads=(brt, bp[6]), writes=(bq32,))
                                vop(lambda e, t0=t0: e.tensor_add(
                                    q32[64:128, t0:t0 + 512], rop_t[64:128, :],
                                    pbk[6][0:64, 0:512]),
                                    reads=(brt, bp[6]), writes=(bq32,))
                            else:
                                vop(lambda e: e.tensor_sub(
                                    w1t[0:64, :], rop_t[0:64, :], pbk[6][64:128, 0:512]),
                                    reads=(brt, bp[6]), writes=(bw1,))
                                vop(lambda e: e.tensor_add(
                                    w1t[64:128, :], rop_t[64:128, :], pbk[6][0:64, 0:512]),
                                    reads=(brt, bp[6]), writes=(bw1,))
                                split(kpr[0][:, t0:t0 + 512], kpr[1][:, t0:t0 + 512],
                                      w1t[:], (bw1,), bkp)
                        split(vfp[0][:], vfp[1][:], pbk[2][:, 0:512], (bp[2],), bvf)
                        for q4 in range(4):
                            for ip in range(2):
                                top(lambda e, ip=ip, q4=q4: e.transpose(
                                    pbk[5][:, 0:64].bitcast(BF16),
                                    vfp[ip][:, q4 * 128:(q4 + 1) * 128], ident_b[:]),
                                    reads=(bvf, bO), writes=(bp[5],))
                                vop(lambda e, ip=ip, q4=q4, t0=t0: e.tensor_copy(
                                    vtm[ip][:, t0 + q4 * 128:t0 + (q4 + 1) * 128],
                                    pbk[5][:, 0:64].bitcast(BF16)),
                                    reads=(bp[5],), writes=(bvt,))
                    for tbp in range(NTS // 2):
                        tbs = (2 * tbp, 2 * tbp + 1)
                        for a in range(2):
                            split(qbp[a][0][:], qbp[a][1][:],
                                  q32[:, tbs[a] * 512:tbs[a] * 512 + 512], (bq32,), bqb[a])
                        njt = 4 * tbs[1] + 4
                        for j in range(njt):
                            acts = [a for a in range(2) if j < 4 * tbs[a] + 4]
                            for a in acts:
                                pbs = 4 + 2 * (j % 2) + a
                                r = 2 * (j % 2) + a
                                tbv = tbs[a]
                                first_j = j == 0
                                last_j = j == 4 * tbv + 3
                                top(lambda e, p=pbs, j=j, a=a: e.matmul(
                                    pbk[p][:, 0:512], kpr[0][:, j * 128:(j + 1) * 128],
                                    qbp[a][0][:], start=True, stop=False),
                                    reads=(bkp, bqb[a]), writes=(bp[pbs],))
                                top(lambda e, p=pbs, j=j, a=a: e.matmul(
                                    pbk[p][:, 0:512], kpr[0][:, j * 128:(j + 1) * 128],
                                    qbp[a][1][:], start=False, stop=False),
                                    reads=(bkp, bqb[a]), writes=(bp[pbs],))
                                top(lambda e, p=pbs, j=j, a=a: e.matmul(
                                    pbk[p][:, 0:512], kpr[1][:, j * 128:(j + 1) * 128],
                                    qbp[a][0][:], start=False, stop=True),
                                    reads=(bkp, bqb[a]), writes=(bp[pbs],))
                                vdiag = j - 4 * tbv
                                if vdiag >= 0:
                                    vop(lambda e, p=pbs, v=vdiag, r=r: e.tensor_tensor(
                                        wq1t[r][:], pbk[p][:, 0:512],
                                        masks4[:, v * 512:(v + 1) * 512], ALU.mult),
                                        reads=(bp[pbs], bO), writes=(bwq1[r],))
                                    sop(lambda e, r=r: e.activation(wq1t[r][:], wq1t[r][:],
                                                                    AF.Square),
                                        reads=(bwq1[r],), writes=(bwq1[r],))
                                else:
                                    sop(lambda e, p=pbs, r=r: e.activation(
                                        wq1t[r][:], pbk[p][:, 0:512], AF.Square),
                                        reads=(bp[pbs],), writes=(bwq1[r],))
                                vop(lambda e, r=r: e.tensor_mul(w2t[r][:], wq1t[r][:],
                                                                wq1t[r][:]),
                                    reads=(bwq1[r],), writes=(bw2_[r],))
                                split(w2pp[r][0][:], w2pp[r][1][:], w2t[r][:], (bw2_[r],),
                                      bw2p[r])
                                top(lambda e, j=j, r=r, a=a: e.matmul(
                                    pbk[2 + a][:, 0:512], vtm[0][:, j * 128:(j + 1) * 128],
                                    w2pp[r][0][:], start=j == 0, stop=False),
                                    reads=(bvt, bw2p[r]), writes=(bp[2 + a],))
                                top(lambda e, j=j, r=r, a=a: e.matmul(
                                    pbk[2 + a][:, 0:512], vtm[0][:, j * 128:(j + 1) * 128],
                                    w2pp[r][1][:], start=False, stop=False),
                                    reads=(bvt, bw2p[r]), writes=(bp[2 + a],))
                                top(lambda e, j=j, r=r, a=a, lj=last_j: e.matmul(
                                    pbk[2 + a][:, 0:512], vtm[1][:, j * 128:(j + 1) * 128],
                                    w2pp[r][0][:], start=False, stop=lj),
                                    reads=(bvt, bw2p[r]), writes=(bp[2 + a],))
                                mm2s(pbk[a][0:1, 0:512], ones_col[:, 0:1],
                                     w2pp[r][0][:], w2pp[r][1][:], bp[a], bO, bw2p[r],
                                     j == 0, last_j)
                        for a in range(2):
                            tbv = tbs[a]
                            t0 = tbv * 512
                            vop(lambda e, a=a: e.tensor_scalar_max(
                                den_s[a][:], pbk[a][0:1, 0:512], 1e-6),
                                reads=(bp[a],), writes=(bden_s[a],))
                            vop(lambda e, a=a: e.reciprocal(den2[:], den_s[a][:]),
                                reads=(bden_s[a],), writes=(bden,))
                            split(denp[0][:], denp[1][:], den2[:], (bden,), bdenp)
                            mm2s(pbk[4][:, 0:512], ones_row[0:1, :], denp[0][:], denp[1][:],
                                 bp[4], bO, bdenp, True, True)
                            sop(lambda e: e.copy(dbc[:], pbk[4][:, 0:512]),
                                reads=(bp[4],), writes=(bdbc,))
                            vop(lambda e, a=a: e.tensor_tensor(ot[:], pbk[2 + a][:, 0:512],
                                                               dbc[:], ALU.mult),
                                reads=(bp[2 + a], bdbc), writes=(bot,))
                            g0 = b * T + t0
                            blk = g0 // L
                            co = g0 % L
                            store(a2a3_in[blk * 256 + hl * 128:blk * 256 + (hl + 1) * 128,
                                          co:co + 512], ot[:], bot, wbuf=bIN3)
                            if debug:
                                store(dbg["o1"][hl * 128:(hl + 1) * 128, g0:g0 + 512],
                                      ot[:], bot)

    emit_consts()
    emit_vprep(wdn_t, vdn_h, vdn_l, True, bVDN)
    rec.barrier()
    emit_vprep(w2_t, v2_h, v2_l, False, bV2)
    rec.barrier()
    emit_p0()
    rec.barrier()
    emit_collective("AllToAll", a2a1_in, a2a1_out, bIN1, bA2A1)
    emit_proj_ln(a2a1_out, bA2A1, wo0_tl, x0loc, None, x1a_s, bX1A, "x1a")
    rec.barrier()
    emit_ffn(x1a_s, bX1A, True, wup_tl, wgt_tl, cs_up, cs_gt, vdn_h, vdn_l, bVDN,
             cs_dn,
             [(lambda oc, c0, tkb: ag2_in[oc * 128:(oc + 1) * 128, c0:c0 + tkb], bIN2),
              (lambda oc, c0, tkb: x1loc_s[oc * 128:(oc + 1) * 128, c0:c0 + tkb], bX1L)],
             "x1")
    rec.barrier()
    emit_collective("AllGather", ag2_in, ag2_out, bIN2, bAG2)
    emit_p2()
    rec.barrier()
    emit_collective("AllToAll", a2a3_in, a2a3_out, bIN3, bA2A3)
    emit_proj_ln(a2a3_out, bA2A3, wo1_tl, x1loc_s, bX1L, x2a_s, bX2A, "x2a")
    rec.barrier()
    emit_ffn(x2a_s, bX2A, False, w3_tl, w1_tl, None, None, v2_h, v2_l, bV2, None,
             [(lambda oc, c0, tkb: y[oc * 128:(oc + 1) * 128, c0:c0 + tkb], bY)],
             None)
    if debug:
        rec.op("scalar", lambda e: e.dma_start(out=dbg["vdnh"][:, :, :, :], in_=vdn_h[:, :, :, :]),
               reads=(bVDN,), writes=(), queue="dA")
        rec.op("scalar", lambda e: e.dma_start(out=dbg["vdnl"][:, :, :, :], in_=vdn_l[:, :, :, :]),
               reads=(bVDN,), writes=(), queue="dA")
        rec.op("scalar", lambda e: e.dma_start(out=dbg["hidh"][:, :, :], in_=hid_h[:, :, :]),
               reads=(bHID,), writes=(), queue="dA")
        rec.op("scalar", lambda e: e.dma_start(out=dbg["hidl"][:, :, :], in_=hid_l[:, :, :]),
               reads=(bHID,), writes=(), queue="dA")
    rec.op("vector", lambda e: e.engine_nop(), reads=(bY,), writes=())

    rec.finalize()
    with stk, nc.Block() as block, contextlib.ExitStack() as semstk:
        sems = {s: semstk.enter_context(nc.semaphore(f"sem_{s}"))
                for s in ENGINES + QUEUES}
        for eng in ENGINES:
            getattr(block, eng)(lambda proxy, eng=eng: rec.emit(eng, proxy, sems))
    return nc



def _hadamard(n):
    h = np.array([[1.0]], dtype=np.float32)
    while h.shape[0] < n:
        h = np.block([[h, h], [h, -h]])
    return h.astype(np.float32)


def _prep_inputs(inputs, B=2, T=8192):
    D = 2048
    INTER = 8192
    L = B * T // NCORES
    x0 = np.asarray(inputs["L0_attn_hidden_states"], dtype=np.float32)
    x0f = np.ascontiguousarray(x0.reshape(B * T, D).T)

    def tl(wt, nich):
        din_, dout = wt.shape
        return np.ascontiguousarray(
            wt.reshape(din_ // 128, 128, dout // 128, 128).transpose(2, 1, 0, 3)
            .reshape(dout // 128, 128, din_))

    WT = {k: np.asarray(v, dtype=np.float32) for k, v in inputs.items()}
    wo0_tl = tl(np.ascontiguousarray(WT["L0_attn_Wo"].T), 16)
    wup_tl = tl(np.ascontiguousarray(WT["L0_ffn_W_up"].T), 64)
    wgt_tl = tl(np.ascontiguousarray(WT["L0_ffn_W_gate"].T), 64)
    wdn_t = np.ascontiguousarray(WT["L0_ffn_W_down"].T)
    wo1_tl = tl(np.ascontiguousarray(WT["L1_attn_Wo"].T), 16)
    w1_tl = tl(np.ascontiguousarray(WT["L1_ffn_W1"].T), 64)
    w3_tl = tl(np.ascontiguousarray(WT["L1_ffn_W3"].T), 64)
    w2_t = np.ascontiguousarray(WT["L1_ffn_W2"].T)

    inv = (10000.0 ** (-(np.arange(64, dtype=np.float64) / 64.0)))
    ang = inv[:, None] * np.arange(T, dtype=np.float64)[None, :]
    cos1 = np.cos(ang).astype(np.float32)
    sin1 = np.sin(ang).astype(np.float32)
    cosT = np.concatenate([cos1, cos1], axis=0)
    sinT = np.concatenate([sin1, sin1], axis=0)
    scale = np.float32(1.0 / np.sqrt(128.0))
    cosqT = (cosT * scale).astype(np.float32)
    sinqT = (sinT * scale).astype(np.float32)

    ident = np.eye(128, dtype=np.float32)
    jj, tt = np.meshgrid(np.arange(128), np.arange(128), indexing="ij")
    hmaskT = (jj <= tt).astype(np.float32)
    m4 = np.zeros((128, 4 * 512), dtype=np.float32)
    for v in range(4):
        j0, t0g = np.meshgrid(np.arange(128), np.arange(512), indexing="ij")
        m4[:, v * 512:(v + 1) * 512] = (v * 128 + j0 <= t0g).astype(np.float32)
    h128 = _hadamard(128)

    su_ug_fm = np.ascontiguousarray(WT["L0_ffn_SU_ug"].reshape(16, 128).T)
    su_dp_fm = np.ascontiguousarray(WT["L0_ffn_SU_dp"].reshape(64, 128).T)
    wsc_up_fm = np.ascontiguousarray(WT["L0_ffn_Wscale_ug"][:INTER].reshape(64, 128).T)
    wsc_gt_fm = np.ascontiguousarray(WT["L0_ffn_Wscale_ug"][INTER:].reshape(64, 128).T)
    wsc_dn_fm = np.ascontiguousarray(WT["L0_ffn_Wscale_dp"].reshape(16, 128).T)

    wq0t = np.ascontiguousarray(WT["L0_attn_Wq"].T)
    wk0t = np.ascontiguousarray(WT["L0_attn_Wk"].T)
    wv0t = np.ascontiguousarray(WT["L0_attn_Wv"].T)
    wg0t = np.ascontiguousarray(WT["L0_attn_Wg"].T)
    wq1t = np.ascontiguousarray(WT["L1_attn_Wq"].T)
    wk1t = np.ascontiguousarray(WT["L1_attn_Wk"].T)
    wv1t = np.ascontiguousarray(WT["L1_attn_Wv"].T)
    ws1t = np.ascontiguousarray(WT["L0_attn_Ws1"].T)

    in_maps = []
    for c in range(NCORES):
        hs = slice(c * 256, (c + 1) * 256)
        m = {
            "x0f": x0f,
            "x0loc": np.ascontiguousarray(x0f[:, c * L:(c + 1) * L]),
            "wq0": np.ascontiguousarray(wq0t[:, hs]),
            "wk0": np.ascontiguousarray(wk0t[:, hs]),
            "wv0": np.ascontiguousarray(wv0t[:, hs]),
            "wg0": np.ascontiguousarray(wg0t[:, hs]),
            "ws1t": ws1t,
            "ws2t": np.ascontiguousarray(WT["L0_attn_Ws2"][c * 128:(c + 1) * 128].T),
            "bs2c": np.ascontiguousarray(WT["L0_attn_bs2"][c * 128:(c + 1) * 128]
                                         .reshape(1, 128)),
            "gnw": np.ascontiguousarray(WT["L0_attn_g_norm_weight"].reshape(128, 1)),
            "wo0_tl": wo0_tl,
            "wup_tl": wup_tl, "wgt_tl": wgt_tl, "wdn_t": wdn_t,
            "su_ug_fm": su_ug_fm, "su_dp_fm": su_dp_fm,
            "wsc_up_fm": wsc_up_fm, "wsc_gt_fm": wsc_gt_fm, "wsc_dn_fm": wsc_dn_fm,
            "wq1": np.ascontiguousarray(wq1t[:, hs]),
            "wk1": np.ascontiguousarray(wk1t[:, hs]),
            "wv1": np.ascontiguousarray(wv1t[:, hs]),
            "wo1_tl": wo1_tl,
            "w1_tl": w1_tl, "w3_tl": w3_tl, "w2_t": w2_t,
            "cosT": cosT, "sinT": sinT, "cosqT": cosqT, "sinqT": sinqT,
            "ident_in": ident, "hmaskT_in": hmaskT, "masks4_in": m4,
            "h128_in": h128,
        }
        in_maps.append(m)
    return in_maps


def kernel(**inputs):
    B, T, D = 2, 8192, 2048
    L = B * T // NCORES
    in_maps = _prep_inputs(inputs, B, T)
    nc = build_nc(B, T, debug=False, paranoid=True)
    res = run_bass_kernel_spmd(nc, in_maps, core_ids=list(range(NCORES)))
    parts = [res.results[c]["y"] for c in range(NCORES)]
    full = np.concatenate(parts, axis=1)
    out = np.ascontiguousarray(full.T).reshape(B, T, D).astype(np.float32)
    return out
